# version 20
# baseline (speedup 1.0000x reference)
import math
from contextlib import ExitStack

import numpy as np

import concourse.bass as bass
import concourse.mybir as mybir
from concourse.bass_utils import run_bass_kernel_spmd

F32 = mybir.dt.float32
BF16 = mybir.dt.bfloat16
F8 = mybir.dt.float8e4
I32 = mybir.dt.int32
AF = mybir.ActivationFunctionType
ALU = mybir.AluOpType
AX = mybir.AxisListType

NCORES = 8
S = 4096
D = 1024
FF = 4096
T = 32
P = 128
HD = 64
E2 = 128

LAMBDA_INIT = 0.8 - 0.6 * math.exp(-0.3 * 12)
SCALE = 1.0 / 64.0 ** 0.25
RMS_EPS = 1e-8
LN_EPS = 1e-5
AM_EPS = 1e-6
LN1MLI = math.log(1.0 - LAMBDA_INIT)

NFCP = 32
NWPP = 32
NPIECE = NFCP + NWPP


def mkap(t, offset, dims):
    ap = t[:]
    if ap.ndim > 2:
        names = [chr(97 + i) for i in range(ap.ndim - 1)]
        pat = "p " + " ".join(names) + " -> p (" + " ".join(names) + ")"
        ap = ap.rearrange(pat)
    while ap.ndim < len(dims):
        ap = ap.unsqueeze(ap.ndim)
    while ap.ndim > len(dims):
        ap = ap.squeeze(ap.ndim - 1)
    for i, (st, c) in enumerate(dims):
        ap.ap[i] = [st, c]
    ap.offset = offset
    return ap


def build_program():
    nc = bass.Bass("TRN2", target_bir_lowering=False)

    x_d = nc.dram_tensor("x", [S, D], F32, kind="ExternalInput").ap()
    wqkv_d = nc.dram_tensor("wqkv", [D, 3 * D], F32, kind="ExternalInput").ap()
    wo_d = nc.dram_tensor("wo", [D, D], F32, kind="ExternalInput").ap()
    wfct_d = nc.dram_tensor("wfct", [D, FF], F32, kind="ExternalInput").ap()
    wpt_d = nc.dram_tensor("wpt", [FF, D], F32, kind="ExternalInput").ap()
    lamv_d = nc.dram_tensor("lamv", [1, 4 * HD], F32, kind="ExternalInput").ap()
    alnw_d = nc.dram_tensor("alnw", [P, E2], F32, kind="ExternalInput").ap()
    alnb_d = nc.dram_tensor("alnb", [P, 1], F32, kind="ExternalInput").ap()
    flnw_d = nc.dram_tensor("flnw", [P, NWPP], F32, kind="ExternalInput").ap()
    flnb_d = nc.dram_tensor("flnb", [P, NWPP], F32, kind="ExternalInput").ap()
    mask_d = nc.dram_tensor("maskc", [P, P], F32, kind="ExternalInput").ap()
    ones_d = nc.dram_tensor("onesc", [1, P], F32, kind="ExternalInput").ap()
    idn_d = nc.dram_tensor("idnc", [P, P], F32, kind="ExternalInput").ap()

    out_d = nc.dram_tensor("out", [S, D], F32, kind="ExternalOutput").ap()

    xp_d = nc.dram_tensor("xp_s", [S, D], F32, kind="Internal").ap()
    wfcq_d = nc.dram_tensor("wfcq_s", [D, FF], F8, kind="Internal").ap()
    wpq_d = nc.dram_tensor("wpq_s", [FF, D], F8, kind="Internal").ap()
    bc_d = nc.dram_tensor("bc_s", [1, 512], F32, kind="Internal").ap()
    vd_d = nc.dram_tensor("vd_s", [2, P, D], BF16, kind="Internal").ap()
    wd_d = nc.dram_tensor("wd_s", [2, P, HD], BF16, kind="Internal").ap()

    with ExitStack() as ctx:
        en = ctx.enter_context

        ones_b = en(nc.sbuf_tensor("ones_b", [1, P], BF16))
        idn_f = en(nc.sbuf_tensor("idn_f", [P, P], F32))
        flnw = en(nc.sbuf_tensor("flnw_s", [P, NWPP], F32))
        flnbb = en(nc.sbuf_tensor("flnbb_s", [P, NWPP], BF16))
        bias2 = en(nc.sbuf_tensor("bias2_s", [1, D], BF16))
        CONSTS = [SCALE, 1.0 / E2, RMS_EPS, -0.5, LN1MLI, LN_EPS]
        cst = en(nc.sbuf_tensor("cst_s", [P, len(CONSTS)], F32))
        for i_, val_ in enumerate(CONSTS):
            nc.const_aps.register(F32, val_, cst[:, i_:i_ + 1])
        pqkv = en(nc.psum_tensor("pqkv", [P, 3072], F32))
        pmid = en(nc.psum_tensor("pmid", [P, 1024], F32))

        sems = {}
        for name in ["s_ld", "s_xte", "s_qkv", "s_qke", "s_ve", "s_ngx", "s_exp", "s_ngx2", "s_exp2", "s_wav", "s_S",
                     "s_av", "s_t1", "s_y", "s_ytr", "s_yte", "s_wo", "s_z1v",
                     "s_ln1v", "s_r1", "s_nm1", "s_xp", "s_wg", "s_thr", "s_q", "s_gm1", "s_gm2", "s_gm3", "s_thrv", "s_cst", "s_fs2",
                     "s_phA",
                     "s_fv", "s_lam1", "s_lam2", "s_lam3", "s_lam4", "s_lam5",
                     "s_lnb1", "s_b1m", "s_b1e",
                     "s_wql", "s_wqf", "s_b2m", "s_b2e",
                     "s_g2", "s_fg", "s_p2",
                     "s_xpte", "s_xptr", "s_fca", "s_fcb",
                     "s_ga", "s_gb", "s_hl",
                     "s_htr", "s_hte", "s_proj", "s_z3v", "s_nm3", "s_out", "s_outd"]:
            sems[name] = en(nc.semaphore(name))
        sm = type("SM", (), sems)
        def semlist(nm, k):
            return [en(nc.semaphore(f"{nm}{i}")) for i in range(k)]
        sxL = semlist("sxl", 4)
        svh1 = semlist("svh1", 2)
        svh2 = semlist("svh2", 2)
        swh1 = semlist("swh1", 2)
        swh2 = semlist("swh2", 2)
        swp1 = semlist("swp1", 4)
        swp2 = semlist("swp2", 4)
        sqdL = semlist("sqdl", 2)
        sxpsL = semlist("sxpsl", 2)
        sxplL = semlist("sxpll", 4)
        soutL = semlist("soutl", 2)
        fv = [0]

        block = en(nc.Block())

        with ExitStack() as actx:
            aen = actx.enter_context
            wqkv = aen(nc.sbuf_tensor("wqkv_s", [P, 8, 3 * D], BF16))
            wo = aen(nc.sbuf_tensor("wo_s", [P, 8, D], BF16))
            maskb = aen(nc.sbuf_tensor("maskb", [P, P], BF16))
            lamv = aen(nc.sbuf_tensor("lamv_s", [1, 4 * HD], F32))
            lsc = aen(nc.sbuf_tensor("lsc_s", [1, 4 * HD], F32))
            lamfin = aen(nc.sbuf_tensor("lamfin_s", [1, 1], F32))
            alnw = aen(nc.sbuf_tensor("alnw_s", [P, E2], F32))
            alnb = aen(nc.sbuf_tensor("alnb_s", [P, 1], F32))
            alnbp = aen(nc.sbuf_tensor("alnbp_s", [P, 1], BF16))
            neglam = aen(nc.sbuf_tensor("neglam_s", [P, 1], F32))
            bias1 = aen(nc.sbuf_tensor("bias1_s", [1, D], BF16))
            NXB = 4
            xb = aen(nc.sbuf_tensor("xb_s", [P, NXB, D], F32))
            xT = aen(nc.sbuf_tensor("xT_s", [P, 2, D], BF16))
            qsb = aen(nc.sbuf_tensor("qsb_s", [P, 2, D], BF16))
            ksb = aen(nc.sbuf_tensor("ksb_s", [P, 2, D], BF16))
            vsb = aen(nc.sbuf_tensor("vsb_s", [P, 2, D], BF16))
            prod = aen(nc.sbuf_tensor("prod_s", [P, 4096], BF16))
            prod2 = aen(nc.sbuf_tensor("prod2_s", [P, 4096], BF16))
            trg1 = aen(nc.sbuf_tensor("trg1_s", [P, 2048], BF16))
            trg2 = aen(nc.sbuf_tensor("trg2_s", [P, 1024], BF16))
            trg3 = aen(nc.sbuf_tensor("trg3_s", [P, 512], F32))
            trg4 = aen(nc.sbuf_tensor("trg4_s", [P, 256], F32))
            trg5 = aen(nc.sbuf_tensor("trg5_s", [P, 128], F32))
            scor = aen(nc.sbuf_tensor("scor_s", [P, 2, 64], F32))
            mxt = aen(nc.sbuf_tensor("mxt_s", [P, 2], F32))
            ngx = aen(nc.sbuf_tensor("ngx_s", [P, 2], F32))
            exw = aen(nc.sbuf_tensor("exw_s", [P, 2, 64], F32))
            zsw = aen(nc.sbuf_tensor("zsw_s", [P, 2, 8], F32))
            zrw = aen(nc.sbuf_tensor("zrw_s", [P, 2, 8], F32))
            aww = aen(nc.sbuf_tensor("aww_s", [P, 2, 64], F32))
            wav = aen(nc.sbuf_tensor("wav_s", [P, 2, HD], BF16))
            Bsc = aen(nc.sbuf_tensor("Bsc_s", [P, 2, HD], BF16))
            Vsc = aen(nc.sbuf_tensor("Vsc_s", [P, 2, D], BF16))
            Ssc = aen(nc.sbuf_tensor("Ssc_s", [P, 2, 8, P], BF16))
            msq = aen(nc.sbuf_tensor("msq_s", [P, 8], F32))
            sqsc = aen(nc.sbuf_tensor("sqsc_s", [P, E2], F32))
            rmsx = aen(nc.sbuf_tensor("rmsx_s", [P, 8], F32))
            t1f = aen(nc.sbuf_tensor("t1f_s", [P, 2, D], F32))
            yT = aen(nc.sbuf_tensor("yT_s", [P, 2, D], BF16))
            z1 = aen(nc.sbuf_tensor("z1_s", [P, 2, D], F32))
            bst = aen(nc.sbuf_tensor("bst_s", [P, 2, 6], F32))
            bag = aen(nc.sbuf_tensor("bag_s", [P, 2, 2], F32))
            rstd = aen(nc.sbuf_tensor("rstd_s", [P, 1], F32))
            nmu = aen(nc.sbuf_tensor("nmu_s", [P, 1], F32))
            xps = aen(nc.sbuf_tensor("xps_s", [P, 2, D], F32))
            wsb = aen(nc.sbuf_tensor("wsb_s", [P, 4, 1024], F32))
            posq = aen(nc.sbuf_tensor("posq_s", [P, 1024], F32))
            redq = aen(nc.sbuf_tensor("redq_s", [P, 1024], F32))
            qout = aen(nc.sbuf_tensor("qout_s", [P, 2, 1024], F8))
            accf = aen(nc.sbuf_tensor("accf_s", [P, 1], F32))
            accp = aen(nc.sbuf_tensor("accp_s", [P, 1], F32))
            redt = aen(nc.sbuf_tensor("redt_s", [P, 1], F32))
            gamrow = aen(nc.sbuf_tensor("gamrow_s", [1, 256], F32))
            thr4row = aen(nc.sbuf_tensor("thr4row_s", [1, 4], F32))
            thr4 = aen(nc.sbuf_tensor("thr4_s", [P, 4], F32))

            P1T = 0
            GAMT = max(1, T // 4)
            P1N = -(-NPIECE // GAMT)
            P2T = GAMT + 1
            P2N = -(-NPIECE // (T + 3 - P2T))

            def _wsrc(pi, dram2):
                if pi < NFCP:
                    return dram2[0].rearrange("(c p) (h q) -> p c h q",
                                              p=P, q=1024)[:, pi // 4, pi % 4, :]
                return dram2[1].rearrange("(c p) q -> p c q", p=P)[:, pi - NFCP, :]

            def _p1_load(sy, pi):
                if pi >= 4:
                    sy.wait_ge(sm.s_wg, pi - 3)
                sy.dma_start(wsb[:, pi % 4, :], _wsrc(pi, (wfct_d, wpt_d))
                             ).then_inc(swp1[pi % 4], 16)

            def _p2_load(sy, pi):
                if pi == 0:
                    sy.wait_ge(sm.s_thr, 16)
                if pi >= 4:
                    sy.wait_ge(sm.s_q, pi - 3)
                sy.dma_start(wsb[:, pi % 4, :], _wsrc(pi, (wfct_d, wpt_d))
                             ).then_inc(swp2[pi % 4], 16)

            def _p2_store(sy, pi):
                sy.wait_ge(sm.s_q, pi + 1)
                sy.dma_start(_wsrc(pi, (wfcq_d, wpq_d)), qout[:, pi % 2, :]
                             ).then_inc(sqdL[pi % 2], 16)

            @block.gpsimd
            def _(g):
                g.dma_start(wqkv[:], wqkv_d.rearrange("(c p) n -> p c n", p=P)
                            ).then_inc(sm.s_ld, 16)
                g.dma_start(wo[:], wo_d.rearrange("(c p) n -> p c n", p=P)
                            ).then_inc(sm.s_ld, 16)
                g.dma_start(maskb[:], mask_d).then_inc(sm.s_ld, 16)
                g.dma_start(ones_b[:], ones_d).then_inc(sm.s_ld, 16)
                g.dma_start(idn_f[:], idn_d).then_inc(sm.s_ld, 16)
                g.dma_start(lamv[:], lamv_d).then_inc(sm.s_ld, 16)
                g.dma_start(alnw[:], alnw_d).then_inc(sm.s_ld, 16)
                g.dma_start(alnb[:], alnb_d).then_inc(sm.s_ld, 16)
                g.dma_start(flnw[:], flnw_d).then_inc(sm.s_ld, 16)
                g.dma_start(flnbb[:], flnb_d).then_inc(sm.s_ld, 16)
                g.wait_ge(sm.s_ld, 16 * 10)
                for i_, val_ in enumerate(CONSTS):
                    g.memset(cst[:, i_:i_ + 1], val_)
                g.engine_nop().then_inc(sm.s_cst, 1)
                fg2 = [0]

                def gfence(inst):
                    inst.then_inc(sm.s_fg, 1)
                    fg2[0] += 1
                    g.wait_ge(sm.s_fg, fg2[0])

                for t in range(T + 2):
                    if P1T <= t < GAMT:
                        for j in range(P1N):
                            pi = (t - P1T) * P1N + j
                            if pi < NPIECE:
                                _p1_load(g, pi)
                    w_ = t - 1
                    if 0 <= w_ < T:
                        g.wait_ge(swh2[w_ % 2], 16 * (w_ // 2 + 1))
                        if w_ >= 2:
                            g.wait_ge(sm.s_av, w_ - 1)
                        inst = None
                        for gq in range(8):
                            in0 = mkap(Bsc, (w_ % 2) * HD + gq * 8,
                                       [(2 * HD, P), (0, 16), (1, 8)])
                            in1 = mkap(maskb, 0, [(P, P), (8, 16), (1, 8)])
                            outp = mkap(Ssc, (w_ % 2) * 8 * P + gq * P,
                                        [(2 * 8 * P, P), (8, 16), (1, 8)])
                            inst = g.tensor_tensor(outp, in0, in1, ALU.mult)
                        inst.then_inc(sm.s_S, 1)
                    uy = t - 2
                    if 0 <= uy < T:
                        g.wait_ge(sm.s_t1, uy + 1)
                        t1v = mkap(t1f, (uy % 2) * D,
                                   [(2 * D, P), (P, 8), (1, P)])
                        gfence(g.tensor_tensor(
                            t1v, t1v,
                            mkap(alnw, 0, [(P, P), (0, 8), (1, P)]),
                            ALU.mult))
                        g.tensor_tensor(
                            t1v, t1v,
                            mkap(rmsx, 0, [(8, P), (1, 8), (0, P)]),
                            ALU.mult).then_inc(sm.s_y, 1)
                    if t < T:
                        g.wait_ge(sm.s_p2, t + 1)
                        if t >= 1:
                            g.wait_ge(sm.s_exp2, t)
                        gfence(g.tensor_tensor(
                            mkap(trg1, 0, [(2048, P), (32, 64), (1, 32)]),
                            mkap(prod2, 0, [(4096, P), (64, 64), (1, 32)]),
                            mkap(prod2, 32, [(4096, P), (64, 64), (1, 32)]),
                            ALU.add))
                        plan = [(trg1, 32, trg2), (trg2, 16, trg3),
                                (trg3, 8, trg4), (trg4, 4, trg5), (trg5, 2, None)]
                        for cur, d, nxt in plan:
                            h_ = d // 2
                            cp = cur[:].ap[0][0]
                            if nxt is None:
                                outp = mkap(scor, 64, [(128, P), (1, 64), (1, 1)])
                            else:
                                npp = nxt[:].ap[0][0]
                                outp = mkap(nxt, 0, [(npp, P), (h_, 64), (1, h_)])
                            gfence(g.tensor_tensor(
                                outp,
                                mkap(cur, 0, [(cp, P), (d, 64), (1, h_)]),
                                mkap(cur, h_, [(cp, P), (d, 64), (1, h_)]),
                                ALU.add))
                        g.engine_nop().then_inc(sm.s_g2, 1)
                    if t >= P2T - 1:
                        for j in range(P2N):
                            pi = (t - (P2T - 1)) * P2N + j
                            if pi < NPIECE:
                                _p2_load(g, pi)
                g.engine_nop().then_inc(sm.s_phA, 1)

            @block.sync
            def _(sy):
                sy.wait_ge(sm.s_lam3, 1)
                sy.dma_start(mkap(bc_d.rearrange("a b -> (a b)"), 300,
                                  [(1, 1), (1, 1)]), lamfin[:]).then_inc(sm.s_lam4, 16)
                sy.wait_ge(sm.s_lam4, 16)
                sy.dma_start(neglam[:],
                             mkap(bc_d.rearrange("a b -> (a b)"), 300,
                                  [(0, P), (1, 1)])).then_inc(sm.s_lam5, 16)
                for t in range(T + 4):
                    if t == GAMT:
                        sy.wait_ge(sm.s_wg, NPIECE)
                        sy.dma_start(mkap(bc_d.rearrange("a b -> (a b)"), 0,
                                          [(1, P), (1, 1)]), accf[:]
                                     ).then_inc(sm.s_gm1, 16)
                        sy.dma_start(mkap(bc_d.rearrange("a b -> (a b)"), P,
                                          [(1, P), (1, 1)]), accp[:]
                                     ).then_inc(sm.s_gm1, 16)
                        sy.wait_ge(sm.s_gm1, 32)
                        sy.dma_start(gamrow[:], bc_d[0:1, 0:256]
                                     ).then_inc(sm.s_gm2, 16)
                        sy.wait_ge(sm.s_thrv, 1)
                        sy.dma_start(mkap(bc_d.rearrange("a b -> (a b)"), 256,
                                          [(1, 1), (1, 4)]), thr4row[:]
                                     ).then_inc(sm.s_gm3, 16)
                        sy.wait_ge(sm.s_gm3, 16)
                        sy.dma_start(thr4[:],
                                     mkap(bc_d.rearrange("a b -> (a b)"), 256,
                                          [(0, P), (1, 4)])).then_inc(sm.s_thr, 16)
                    if t >= P2T - 1:
                        for j in range(P2N):
                            pi = (t - (P2T - 1)) * P2N + j
                            if 0 <= pi - 2 < NPIECE - 2:
                                _p2_store(sy, pi - 2)
                            if pi == NPIECE - 1:
                                _p2_store(sy, NPIECE - 2)
                                _p2_store(sy, NPIECE - 1)
                    if t < T:
                        sy.wait_ge(sm.s_ve, t + 1)
                        if t >= 2:
                            sy.wait_ge(svh2[t % 2], 16 * (t // 2))
                        sy.dma_start(vd_d[t % 2, :, :], vsb[:, t % 2, :]
                                     ).then_inc(svh1[t % 2], 16)
                        sy.wait_ge(svh1[t % 2], 16 * (t // 2 + 1))
                        if t >= 2:
                            sy.wait_ge(sm.s_av, t - 1)
                        vsrc = mkap(vd_d.rearrange("a b c -> (a b c)"),
                                    (t % 2) * P * D, [(128, P), (16384, 8), (1, E2)])
                        sy.dma_start(Vsc[:, t % 2, :], vsrc).then_inc(svh2[t % 2], 16)
                sy.wait_ge(svh2[0], 16 * ((T + 1) // 2))
                sy.wait_ge(svh2[1], 16 * (T // 2)).then_inc(sm.s_phA, 1)

            fs2 = [0]

            @block.scalar
            def _(s):
                def sfence(inst):
                    inst.then_inc(sm.s_fs2, 1)
                    fs2[0] += 1
                    s.wait_ge(sm.s_fs2, fs2[0])
                s.wait_ge(sm.s_cst, 1)
                for j in range(2):
                    s.dma_start(xb[:, j % NXB, :], x_d[j * P:(j + 1) * P, :]
                                ).then_inc(sxL[j % 4], 16)
                s.wait_ge(sm.s_lam1, 1)
                s.activation(lsc[0:1, 130:131], lsc[0:1, 128:129], AF.Exp)
                s.activation(lsc[0:1, 131:132], lsc[0:1, 129:130], AF.Exp
                             ).then_inc(sm.s_lam2, 1)
                s.wait_ge(sm.s_b1m, 1)
                s.activation(bias1[:], pmid[0:1, 0:1024], AF.Copy).then_inc(sm.s_b1e, 1)
                s.wait_ge(sm.s_xptr, 1)
                s.activation(xT[:, 0, :], pmid[:], AF.Copy).then_inc(sm.s_xte, 1)

                for t in range(T + 4):
                    u = t - 2
                    v_ = t + 1
                    if 1 <= v_ < T:
                        s.wait_ge(sm.s_xptr, v_ + 1)
                        if v_ >= 2:
                            s.wait_ge(sm.s_qkv, 2 * (v_ - 1))
                        s.activation(xT[:, v_ % 2, :], pmid[:], AF.Copy
                                     ).then_inc(sm.s_xte, 1)
                    if 0 <= u < T:
                        s.wait_ge(sm.s_av, u + 1)
                        if u >= 2:
                            s.wait_ge(sm.s_ytr, u - 1)
                        s.activation(t1f[:, u % 2, :], pmid[:], AF.Copy)
                        for gq in range(8):
                            sfence(s.activation(sqsc[:], pmid[:, gq * P:(gq + 1) * P],
                                                AF.Square,
                                                accum_out=msq[:, gq:gq + 1]))
                        sfence(s.activation(rmsx[:], msq[:], AF.Ln, scale=1.0 / E2,
                                            bias=RMS_EPS))
                        s.activation(rmsx[:], rmsx[:], AF.Exp, scale=-0.5,
                                     bias=LN1MLI).then_inc(sm.s_t1, 1)
                    if t < T:
                        s.wait_ge(sm.s_qkv, 2 * t + 1)
                        if t >= 2:
                            s.wait_ge(sm.s_ngx, t - 1)
                            s.wait_ge(sm.s_g2, t - 1)
                            s.wait_ge(svh1[t % 2], 16 * (t // 2))
                        s.activation(qsb[:, t % 2, :], pqkv[:, 0:1024], AF.Copy)
                        s.activation(ksb[:, t % 2, :], pqkv[:, 1024:2048], AF.Copy
                                     ).then_inc(sm.s_qke, 1)
                        s.wait_ge(sm.s_qkv, 2 * t + 2)
                        s.activation(vsb[:, t % 2, :], pqkv[:, 2048:3072], AF.Copy
                                     ).then_inc(sm.s_ve, 1)
                    u3 = t - 3
                    if 0 <= u3 < T:
                        s.wait_ge(sm.s_ln1v, u3 + 1)
                        sfence(s.activation(rstd[:], bag[:, u3 % 2, 1:2], AF.Ln,
                                            bias=LN_EPS))
                        s.activation(rstd[:], rstd[:], AF.Exp, scale=-0.5
                                     ).then_inc(sm.s_r1, 1)
                        s.wait_ge(sm.s_nm1, u3 + 1)
                        if u3 >= 2:
                            s.wait_ge(sxpsL[u3 % 2], 16 * (u3 // 2))
                        s.activation(xps[:, u3 % 2, :], z1[:, u3 % 2, :], AF.Identity,
                                     scale=rstd[:], bias=nmu[:]).then_inc(sm.s_xp, 1)
                        s.wait_ge(sm.s_xp, u3 + 1)
                        s.dma_start(xp_d[u3 * P:(u3 + 1) * P, :], xps[:, u3 % 2, :]
                                    ).then_inc(sxpsL[u3 % 2], 16)
                    if 0 <= u < T:
                        s.wait_ge(sm.s_ytr, u + 1)
                        if u >= 2:
                            s.wait_ge(sm.s_wo, u - 1)
                        s.activation(yT[:, u % 2, :], pmid[:], AF.Copy
                                     ).then_inc(sm.s_yte, 1)
                    if t < T:
                        s.wait_ge(sm.s_ngx, t + 1)
                        s.activation(exw[:, 0, :], scor[:, 0, :],
                                     AF.Exp, scale=SCALE, bias=ngx[:, 0:1]
                                     ).then_inc(sm.s_exp, 1)
                    if t + 2 < T:
                        if t + 2 >= NXB:
                            s.wait_ge(sm.s_z1v, t + 2 - NXB + 1)
                        s.dma_start(xb[:, (t + 2) % NXB, :],
                                    x_d[(t + 2) * P:(t + 3) * P, :]
                                    ).then_inc(sxL[(t + 2) % 4], 16)
                    if t < T:
                        s.wait_ge(sm.s_ngx2, t + 1)
                        s.activation(exw[:, 1, :], scor[:, 1, :],
                                     AF.Exp, scale=SCALE, bias=ngx[:, 1:2]
                                     ).then_inc(sm.s_exp2, 1)
                        s.wait_ge(sm.s_wav, t + 1)
                        if t >= 2:
                            s.wait_ge(swh2[t % 2], 16 * (t // 2))
                        s.dma_start(wd_d[t % 2, :, :], wav[:, t % 2, :]
                                    ).then_inc(swh1[t % 2], 16)
                        s.wait_ge(swh1[t % 2], 16 * (t // 2 + 1))
                        if t >= 2:
                            s.wait_ge(sm.s_S, t - 1)
                        wsrc = mkap(wd_d.rearrange("a b c -> (a b c)"),
                                    (t % 2) * P * HD, [(8, P), (1024, 8), (1, 8)])
                        s.dma_start(Bsc[:, t % 2, :], wsrc).then_inc(swh2[t % 2], 16)
                if True:
                    s.wait_ge(sxpsL[0], 16 * ((T + 1) // 2))
                    s.wait_ge(sxpsL[1], 16 * (T // 2)).then_inc(sm.s_phA, 1)

            @block.vector
            def _(v):
                def fence(inst):
                    inst.then_inc(sm.s_fv, 1)
                    fv[0] += 1
                    v.wait_ge(sm.s_fv, fv[0])

                v.wait_ge(sm.s_ld, 16 * 10)
                v.tensor_tensor(lsc[0:1, 0:64], lamv[0:1, 0:64],
                                lamv[0:1, 64:128], ALU.mult)
                fence(v.tensor_tensor(lsc[0:1, 64:128], lamv[0:1, 128:192],
                                      lamv[0:1, 192:256], ALU.mult))
                v.reduce_sum(lsc[0:1, 128:129], lsc[0:1, 0:64], AX.X)
                fence(v.reduce_sum(lsc[0:1, 129:130], lsc[0:1, 64:128], AX.X))
                v.engine_nop().then_inc(sm.s_lam1, 1)
                v.wait_ge(sm.s_lam2, 1)
                fence(v.tensor_sub(lsc[0:1, 132:133], lsc[0:1, 131:132],
                                   lsc[0:1, 130:131]))
                fence(v.tensor_scalar(lamfin[:], lsc[0:1, 132:133], -LAMBDA_INIT,
                                      None, ALU.add))
                v.engine_nop().then_inc(sm.s_lam3, 1)
                v.memset(accf[:], 0.0)
                v.memset(accp[:], 0.0)
                v.tensor_scalar(alnbp[:], alnb[:], 1.0 - LAMBDA_INIT, None,
                                ALU.mult).then_inc(sm.s_lnb1, 1)

                for t in range(T + 4):
                    if P1T <= t < GAMT:
                        for j in range(P1N):
                            pi = (t - P1T) * P1N + j
                            if pi < NPIECE:
                                v.wait_ge(swp1[pi % 4], 16 * (pi // 4 + 1))
                                acc = accf if pi < NFCP else accp
                                fence(v.reduce_sum(redt[:], wsb[:, pi % 4, :], AX.X,
                                                   apply_absolute_value=True))
                                fence(v.tensor_tensor(acc[:], acc[:], redt[:],
                                                      ALU.add))
                                v.engine_nop().then_inc(sm.s_wg, 1)
                    if t >= P2T:
                        for j in range(P2N):
                            pi = (t - P2T) * P2N + j
                            if pi < NPIECE:
                                v.wait_ge(swp2[pi % 4], 16 * (pi // 4 + 1))
                                co = 0 if pi < NFCP else 2
                                if pi == 0:
                                    v.wait_ge(sm.s_thr, 16)
                                if pi >= 2:
                                    v.wait_ge(sqdL[pi % 2], 16 * (pi // 2))
                                v.tensor_scalar(posq[:], wsb[:, pi % 4, :],
                                                thr4[:, co:co + 1], None, ALU.is_gt)
                                fence(v.tensor_scalar(redq[:], wsb[:, pi % 4, :],
                                                      thr4[:, co + 1:co + 2], -1.0,
                                                      ALU.is_gt, ALU.add))
                                fence(v.tensor_tensor(qout[:, pi % 2, :], posq[:],
                                                      redq[:], ALU.add))
                                v.engine_nop().then_inc(sm.s_q, 1)
                    if t == GAMT:
                        v.wait_ge(sm.s_gm2, 16)
                        fence(v.reduce_sum(thr4row[0:1, 0:1], gamrow[0:1, 0:P],
                                           AX.X))
                        fence(v.reduce_sum(thr4row[0:1, 2:3], gamrow[0:1, P:2 * P],
                                           AX.X))
                        fence(v.tensor_scalar(thr4row[0:1, 0:1], thr4row[0:1, 0:1],
                                              0.5 / (D * FF), 0.5 * AM_EPS,
                                              ALU.mult, ALU.add))
                        fence(v.tensor_scalar(thr4row[0:1, 2:3], thr4row[0:1, 2:3],
                                              0.5 / (D * FF), 0.5 * AM_EPS,
                                              ALU.mult, ALU.add))
                        fence(v.tensor_scalar(thr4row[0:1, 1:2], thr4row[0:1, 0:1],
                                              -1.0, None, ALU.mult))
                        fence(v.tensor_scalar(thr4row[0:1, 3:4], thr4row[0:1, 2:3],
                                              -1.0, None, ALU.mult))
                        v.engine_nop().then_inc(sm.s_thrv, 1)
                    u3v = t - 3
                    if 0 <= u3v < T:
                        v.wait_ge(sm.s_r1, u3v + 1)
                        if u3v >= 1:
                            v.wait_ge(sm.s_xp, u3v)
                        v.scalar_tensor_tensor(nmu[:], bag[:, u3v % 2, 0:1], -1.0,
                                               rstd[:], ALU.mult, ALU.mult
                                               ).then_inc(sm.s_nm1, 1)
                    uy = t - 2
                    if t < T:
                        v.wait_ge(sm.s_qke, t + 1)
                        if t >= 1:
                            v.wait_ge(sm.s_g2, t)
                        out4b = mkap(prod2, 0,
                                     [(4096, P), (512, 8), (64, 8), (1, 64)])
                        in0b = mkap(qsb, (t % 2) * D + 512,
                                    [(2 * D, P), (0, 8), (64, 8), (1, 64)])
                        in1b = mkap(ksb, (t % 2) * D + 512,
                                    [(2 * D, P), (64, 8), (0, 8), (1, 64)])
                        v.tensor_tensor(out4b, in0b, in1b, ALU.mult
                                        ).then_inc(sm.s_p2, 1)
                        out4 = mkap(prod, 0,
                                    [(4096, P), (512, 8), (64, 8), (1, 64)])
                        in0 = mkap(qsb, (t % 2) * D,
                                   [(2 * D, P), (0, 8), (64, 8), (1, 64)])
                        in1 = mkap(ksb, (t % 2) * D,
                                   [(2 * D, P), (64, 8), (0, 8), (1, 64)])
                        fence(v.tensor_tensor(out4, in0, in1, ALU.mult))
                        fence(v.tensor_reduce(
                            scor[:, 0, :],
                            mkap(prod, 0, [(4096, P), (64, 64), (1, 64)]),
                            AX.X, ALU.add))
                        fence(v.tensor_reduce(mxt[:, 0:1], scor[:, 0, :],
                                              AX.X, ALU.max))
                        v.tensor_scalar(ngx[:, 0:1], mxt[:, 0:1], -SCALE, None,
                                        ALU.mult).then_inc(sm.s_ngx, 1)
                    if 0 <= uy < T:
                        v.wait_ge(sm.s_wo, uy + 1)
                        v.wait_ge(sxL[uy % 4], 16 * (uy // 4 + 1))
                        if uy >= 2:
                            v.wait_ge(sm.s_xp, uy - 1)
                        fence(v.tensor_tensor(z1[:, uy % 2, :], pqkv[:, 0:1024],
                                              xb[:, uy % NXB, :], ALU.add))
                        v.engine_nop().then_inc(sm.s_z1v, 1)
                        if uy >= 2:
                            v.wait_ge(sm.s_r1, uy - 1)
                        v.bn_stats(bst[:, 0, :], z1[:, uy % 2, 0:512])
                        fence(v.bn_stats(bst[:, 1, :], z1[:, uy % 2, 512:1024]))
                        v.bn_aggr(bag[:, uy % 2, :], bst[:]).then_inc(sm.s_ln1v, 1)
                    if t < T:
                        v.wait_ge(sm.s_exp, t + 1)
                        exi1 = mkap(exw, 0, [(128, P), (1, 8), (8, 8)])
                        fence(v.tensor_reduce(zsw[:, 0, :], exi1, AX.X, ALU.add))
                        fence(v.reciprocal(zrw[:, 0, :], zsw[:, 0, :]))
                        fence(v.tensor_tensor(
                            mkap(aww, 0, [(128, P), (8, 8), (1, 8)]),
                            mkap(exw, 0, [(128, P), (8, 8), (1, 8)]),
                            mkap(zrw, 0, [(16, P), (0, 8), (1, 8)]), ALU.mult))
                        v.wait_ge(sm.s_g2, t + 1)
                        fence(v.tensor_reduce(mxt[:, 1:2], scor[:, 1, :],
                                              AX.X, ALU.max))
                        v.tensor_scalar(ngx[:, 1:2], mxt[:, 1:2], -SCALE, None,
                                        ALU.mult).then_inc(sm.s_ngx2, 1)
                        v.wait_ge(sm.s_exp2, t + 1)
                        exi2 = mkap(exw, 64, [(128, P), (1, 8), (8, 8)])
                        fence(v.tensor_reduce(zsw[:, 1, :], exi2, AX.X, ALU.add))
                        fence(v.reciprocal(zrw[:, 1, :], zsw[:, 1, :]))
                        fence(v.tensor_tensor(
                            mkap(aww, 64, [(128, P), (8, 8), (1, 8)]),
                            mkap(exw, 64, [(128, P), (8, 8), (1, 8)]),
                            mkap(zrw, 8, [(16, P), (0, 8), (1, 8)]), ALU.mult))
                        if t == 0:
                            v.wait_ge(sm.s_lam5, 16)
                        if t >= 2:
                            v.wait_ge(swh1[t % 2], 16 * (t // 2))
                        v.scalar_tensor_tensor(wav[:, t % 2, :], aww[:, 1, :],
                                               neglam[:], aww[:, 0, :],
                                               ALU.mult, ALU.add).then_inc(sm.s_wav, 1)

                v.engine_nop().then_inc(sm.s_phA, 1)

            @block.tensor
            def _(te):
                te.wait_ge(sm.s_ld, 16 * 10)
                te.wait_ge(sm.s_lnb1, 1)
                te.wait_ge(sm.s_lam5, 1)
                inst = None
                for half in range(2):
                    for c in range(8):
                        inst = te.matmul(pmid[0:1, half * 512:(half + 1) * 512],
                                         alnbp[:],
                                         wo[:, c, half * 512:(half + 1) * 512],
                                         start=(c == 0), stop=(c == 7))
                inst.then_inc(sm.s_b1m, 1)
                te.wait_ge(sm.s_b1e, 1)
                te.wait_ge(sxL[0], 16)
                inst = None
                for c in range(8):
                    inst = te.transpose(pmid[:, c * P:(c + 1) * P],
                                        xb[:, 0, c * P:(c + 1) * P], idn_f[:])
                inst.then_inc(sm.s_xptr, 1)

                for t in range(T + 4):
                    u = t - 2
                    v_ = t + 1
                    if 1 <= v_ < T:
                        te.wait_ge(sxL[v_ % 4], 16 * (v_ // 4 + 1))
                        te.wait_ge(sm.s_xte, min(t + 1, T))
                        if u >= 1:
                            te.wait_ge(sm.s_yte, u)
                        inst = None
                        for c in range(8):
                            inst = te.transpose(pmid[:, c * P:(c + 1) * P],
                                                xb[:, v_ % NXB, c * P:(c + 1) * P],
                                                idn_f[:])
                        inst.then_inc(sm.s_xptr, 1)
                    if 0 <= u < T:
                        te.wait_ge(sm.s_S, u + 1)
                        te.wait_ge(svh2[u % 2], 16 * (u // 2 + 1))
                        te.wait_ge(sm.s_xte, min(t + 2, T))
                        inst = None
                        for gq in range(8):
                            inst = te.matmul(pmid[:, gq * P:(gq + 1) * P],
                                             Ssc[:, u % 2, gq, :],
                                             Vsc[:, u % 2, gq * P:(gq + 1) * P],
                                             start=True, stop=True)
                        inst.then_inc(sm.s_av, 1)
                    if t < T:
                        te.wait_ge(sm.s_xte, t + 1)
                        if t >= 1:
                            te.wait_ge(sm.s_ve, t)
                        if t >= 3:
                            te.wait_ge(sm.s_z1v, t - 2)
                        inst = None
                        for c in range(8):
                            for piece in range(6):
                                inst = te.matmul(
                                    pqkv[:, piece * 512:(piece + 1) * 512],
                                    xT[:, t % 2, c * P:(c + 1) * P],
                                    wqkv[:, c, piece * 512:(piece + 1) * 512],
                                    start=(c == 0), stop=(c == 7))
                                if c == 7 and piece == 3:
                                    inst.then_inc(sm.s_qkv, 1)
                        inst.then_inc(sm.s_qkv, 1)
                    if 0 <= u < T:
                        te.wait_ge(sm.s_y, u + 1)
                        inst = None
                        for c in range(8):
                            inst = te.transpose(pmid[:, c * P:(c + 1) * P],
                                                t1f[:, u % 2, c * P:(c + 1) * P],
                                                idn_f[:])
                        inst.then_inc(sm.s_ytr, 1)
                        te.wait_ge(sm.s_yte, u + 1)
                        te.wait_ge(sm.s_qke, min(t + 1, T))
                        inst = None
                        for half in range(2):
                            te.matmul(pqkv[:, half * 512:(half + 1) * 512], ones_b[:],
                                      bias1[0:1, half * 512:(half + 1) * 512],
                                      start=True, stop=False)
                            for n in range(8):
                                lhs = mkap(yT, (u % 2) * D + n,
                                           [(2 * D, P), (E2, 8), (8, 16)])
                                inst = te.matmul(
                                    pqkv[:, half * 512:(half + 1) * 512], lhs,
                                    wo[:, n, half * 512:(half + 1) * 512],
                                    start=False, stop=(n == 7))
                        inst.then_inc(sm.s_wo, 1)
                te.wait_ge(sm.s_qkv, 2 * T).then_inc(sm.s_phA, 1)

        with ExitStack() as bctx:
            ben = bctx.enter_context
            wfcq = ben(nc.sbuf_tensor("wfcq_r", [P, 8, FF], F8))
            wpq = ben(nc.sbuf_tensor("wpq_r", [P, NWPP, D], F8))
            NXP = 4
            NHF = 3
            xpb = ben(nc.sbuf_tensor("xpb_s", [P, NXP, D], F32))
            xpT = ben(nc.sbuf_tensor("xpT_s", [P, 3, D], BF16))
            hf = ben(nc.sbuf_tensor("hf_s", [P, NHF, FF], F32))
            hT = ben(nc.sbuf_tensor("hT_s", [P, 2, NWPP, P], BF16))
            z3 = ben(nc.sbuf_tensor("z3_s", [P, 2, D], F32))
            bst2 = ben(nc.sbuf_tensor("bst2_s", [P, 8, 6], F32))
            bag2 = ben(nc.sbuf_tensor("bag2_s", [P, 2], F32))
            r2 = ben(nc.sbuf_tensor("r2_s", [P, 1], F32))
            nm2 = ben(nc.sbuf_tensor("nm2_s", [P, 1], F32))
            bst3 = ben(nc.sbuf_tensor("bst3_s", [P, 2, 6], F32))
            bag3 = ben(nc.sbuf_tensor("bag3_s", [P, 2], F32))
            r3 = ben(nc.sbuf_tensor("r3_s", [P, 2], F32))
            nm3 = ben(nc.sbuf_tensor("nm3_s", [P, 2], F32))
            outb = ben(nc.sbuf_tensor("outb_s", [P, 2, D], F32))
            nwi = ben(nc.sbuf_tensor("nwi_s", [P, 1], I32))
            nwy = ben(nc.sbuf_tensor("nwy_s", [P, 1], F32))
            nwv = ben(nc.sbuf_tensor("nwv_s", [P, 1], F32))
            nwh = ben(nc.sbuf_tensor("nwh_s", [P, 1], F32))
            nwa = ben(nc.sbuf_tensor("nwa_s", [P, 1], F32))
            nwb = ben(nc.sbuf_tensor("nwb_s", [P, 1], F32))

            pHT = pqkv[:, 0:1024]
            pPJ = pqkv[:, 1024:2048]
            pFA = pqkv[:, 2048:3072]
            pFB = pmid[:, 0:1024]

            @block.sync
            def _(sy):
                sy.wait_ge(sm.s_phA, 5)
                sy.wait_ge(sqdL[0], 16 * (NPIECE // 2))
                sy.wait_ge(sqdL[1], 16 * (NPIECE // 2))
                for c in range(8):
                    sy.dma_start(wfcq[:, c, :],
                                 wfcq_d.rearrange("(c p) n -> p c n", p=P)[:, c, :]
                                 ).then_inc(sm.s_wqf, 16)
                for c in range(NWPP):
                    sy.dma_start(wpq[:, c, :],
                                 wpq_d.rearrange("(c p) n -> p c n", p=P)[:, c, :]
                                 ).then_inc(sm.s_wql, 16)
                for t in range(T):
                    sy.wait_ge(sxpsL[t % 2], 16 * (t // 2 + 1))
                    if t >= 4:
                        sy.wait_ge(sm.s_z3v, t - 3)
                    sy.dma_start(xpb[:, t % NXP, :], xp_d[t * P:(t + 1) * P, :]
                                 ).then_inc(sxplL[t % 4], 16)
                sy.wait_ge(soutL[0], 16 * ((T + 1) // 2))
                sy.wait_ge(soutL[1], 16 * (T // 2))

            @block.gpsimd
            def _(g):
                g.wait_ge(sm.s_phA, 5)
                for w3 in range(T):
                    g.wait_ge(sm.s_out, w3 + 1)
                    g.dma_start(out_d[w3 * P:(w3 + 1) * P, :], outb[:, w3 % 2, :]
                                ).then_inc(soutL[w3 % 2], 16)

            @block.scalar
            def _(s):
                s.wait_ge(sm.s_phA, 5)
                s.wait_ge(sm.s_xptr, T + 1)
                s.activation(xpT[:, 0, :], pHT, AF.Copy).then_inc(sm.s_xpte, 1)
                s.wait_ge(sm.s_xptr, T + 2)
                s.activation(xpT[:, 1, :], pHT, AF.Copy).then_inc(sm.s_xpte, 1)
                for t in range(T + 5):
                    u = t - 1
                    if t < T:
                        s.wait_ge(sm.s_fca, t + 1)
                        if t >= NHF:
                            s.wait_ge(sm.s_htr, 8 * (t - NHF) + 8)
                        s.activation(hf[:, t % NHF, 0:1024], pFA, AF.Gelu)
                        s.activation(hf[:, t % NHF, 1024:2048], pFB, AF.Gelu
                                     ).then_inc(sm.s_ga, 1)
                    if t == 1:
                        s.wait_ge(sm.s_b2m, NWPP)
                        s.activation(bias2[:], pPJ[0:1, :], AF.Copy
                                     ).then_inc(sm.s_b2e, 1)
                    if 0 <= u < T:
                        if u >= 2:
                            s.wait_ge(sm.s_proj, u - 1)
                        for r_ in range(8):
                            s.wait_ge(sm.s_htr, 8 * u + r_ + 1)
                            inst = None
                            for j in range(4):
                                c = 4 * r_ + j
                                inst = s.activation(
                                    hT[:, u % 2, c, :],
                                    pqkv[:, (r_ % 2) * 512 + j * P:
                                         (r_ % 2) * 512 + (j + 1) * P],
                                    AF.Copy, scale=flnw[:, c:c + 1])
                            inst.then_inc(sm.s_hte, 1)
                    if t < T:
                        s.wait_ge(sm.s_fcb, t + 1)
                        s.activation(hf[:, t % NHF, 2048:3072], pFA, AF.Gelu)
                        s.activation(hf[:, t % NHF, 3072:4096], pFB, AF.Gelu
                                     ).then_inc(sm.s_gb, 1)
                    v2 = t + 2
                    if 2 <= v2 < T:
                        s.wait_ge(sm.s_xptr, T + 3 + t)
                        s.activation(xpT[:, v2 % 3, :], pHT, AF.Copy
                                     ).then_inc(sm.s_xpte, 1)
                    w3 = t - 3
                    if 0 <= w3 < T:
                        s.wait_ge(sm.s_nm3, w3 + 1)
                        if w3 >= 2:
                            s.wait_ge(soutL[w3 % 2], 16 * (w3 // 2))
                        s.activation(outb[:, w3 % 2, :], z3[:, w3 % 2, :],
                                     AF.Identity, scale=r3[:, w3 % 2:w3 % 2 + 1],
                                     bias=nm3[:, w3 % 2:w3 % 2 + 1]
                                     ).then_inc(sm.s_out, 1)

            @block.vector
            def _(v):
                def fence(inst):
                    inst.then_inc(sm.s_fv, 1)
                    fv[0] += 1
                    v.wait_ge(sm.s_fv, fv[0])

                def newton_rsqrt(var_ap, out_r, out_nm, mean_ap):
                    fence(v.tensor_scalar(nwv[:], var_ap, LN_EPS, None, ALU.add))
                    fence(v.tensor_scalar(nwi[:], nwv[:].bitcast(I32), 1,
                                          -1, ALU.logical_shift_right,
                                          ALU.bitwise_xor))
                    fence(v.tensor_scalar(nwi[:], nwi[:], 0x5f3759df + 1, None,
                                          ALU.add))
                    fence(v.tensor_copy(nwy[:], nwi[:].bitcast(F32)))
                    fence(v.tensor_scalar(nwh[:], nwv[:], -0.5, None, ALU.mult))
                    for it in range(3):
                        fence(v.tensor_tensor(nwa[:], nwy[:], nwy[:], ALU.mult))
                        fence(v.tensor_scalar(nwb[:], nwa[:], nwh[:], 1.5,
                                              ALU.mult, ALU.add))
                        fence(v.tensor_tensor(out_r if it == 2 else nwy[:],
                                              nwy[:], nwb[:], ALU.mult))
                    fence(v.scalar_tensor_tensor(out_nm, mean_ap, -1.0, out_r,
                                                 ALU.mult, ALU.mult))

                v.wait_ge(sm.s_phA, 5)
                for t in range(T + 5):
                    if t < T:
                        v.wait_ge(sm.s_ga, t + 1)
                        for c in range(4):
                            v.bn_stats(bst2[:, c, :],
                                       hf[:, t % NHF, c * 512:(c + 1) * 512])
                    w = t - 2
                    if 0 <= w < T:
                        v.wait_ge(sm.s_proj, w + 1)
                        v.wait_ge(sxplL[w % 4], 16 * (w // 4 + 1))
                        if w >= 2:
                            v.wait_ge(sm.s_out, w - 1)
                        fence(v.tensor_tensor(z3[:, w % 2, :], pPJ,
                                              xpb[:, w % NXP, :], ALU.add))
                        v.engine_nop().then_inc(sm.s_z3v, 1)
                        v.bn_stats(bst3[:, 0, :], z3[:, w % 2, 0:512])
                        fence(v.bn_stats(bst3[:, 1, :], z3[:, w % 2, 512:1024]))
                        fence(v.bn_aggr(bag3[:], bst3[:]))
                        newton_rsqrt(bag3[:, 1:2], r3[:, w % 2:w % 2 + 1],
                                     nm3[:, w % 2:w % 2 + 1], bag3[:, 0:1])
                        v.engine_nop().then_inc(sm.s_nm3, 1)
                    if t < T:
                        v.wait_ge(sm.s_gb, t + 1)
                        inst = None
                        for c in range(4, 8):
                            inst = v.bn_stats(bst2[:, c, :],
                                              hf[:, t % NHF, c * 512:(c + 1) * 512])
                        fence(inst)
                        fence(v.bn_aggr(bag2[:], bst2[:]))
                        newton_rsqrt(bag2[:, 1:2], r2[:], nm2[:], bag2[:, 0:1])
                        fence(v.tensor_scalar(hf[:, t % NHF, :], hf[:, t % NHF, :],
                                              r2[:], nm2[:], ALU.mult, ALU.add))
                        v.engine_nop().then_inc(sm.s_hl, 1)

            @block.tensor
            def _(te):
                te.wait_ge(sm.s_phA, 5)
                te.wait_ge(sm.s_ve, T)
                te.wait_ge(sm.s_wqf, 16 * 8)
                te.wait_ge(sm.s_z1v, T)
                for v0 in range(2):
                    te.wait_ge(sxplL[v0], 16)
                    if v0 == 1:
                        te.wait_ge(sm.s_xpte, 1)
                    inst = None
                    for c in range(8):
                        inst = te.transpose(pqkv[:, c * P:(c + 1) * P],
                                            xpb[:, v0, c * P:(c + 1) * P],
                                            idn_f[:])
                    inst.then_inc(sm.s_xptr, 1)

                for t in range(T + 5):
                    u = t - 1
                    w = t - 2
                    if t < T:
                        te.wait_ge(sm.s_xpte, t + 1)
                        if t == 0:
                            te.wait_ge(sm.s_yte, T)
                            te.wait_ge(sm.s_xte, T)
                        if t >= 1:
                            te.wait_ge(sm.s_gb, t)
                        inst = None
                        for c in range(8):
                            for piece in range(2):
                                te.matmul(
                                    pFA[:, piece * 512:(piece + 1) * 512],
                                    xpT[:, t % 3, c * P:(c + 1) * P],
                                    wfcq[:, c, piece * 512:(piece + 1) * 512],
                                    start=(c == 0), stop=(c == 7))
                            for piece in range(2):
                                inst = te.matmul(
                                    pFB[:, piece * 512:(piece + 1) * 512],
                                    xpT[:, t % 3, c * P:(c + 1) * P],
                                    wfcq[:, c, 1024 + piece * 512:
                                         1024 + (piece + 1) * 512],
                                    start=(c == 0), stop=(c == 7))
                        inst.then_inc(sm.s_fca, 1)
                    if t == 1:
                        te.wait_ge(sm.s_wql, 16 * NWPP)
                        inst = None
                        for c in range(NWPP):
                            for half in range(2):
                                inst = te.matmul(pqkv[0:1, 1024 + half * 512:
                                                      1024 + (half + 1) * 512],
                                                 flnbb[:, c:c + 1],
                                                 wpq[:, c,
                                                     half * 512:(half + 1) * 512],
                                                 start=(c == 0),
                                                 stop=(c == NWPP - 1))
                            inst.then_inc(sm.s_b2m, 1)
                    if 0 <= u < T or 0 <= w < T:
                        if 0 <= w < T:
                            te.wait_ge(sm.s_hte, 8 * w + 8)
                            if w == 0:
                                te.wait_ge(sm.s_b2e, 1)
                            if w >= 1:
                                te.wait_ge(sm.s_z3v, w)
                            for half in range(2):
                                te.matmul(pPJ[:, half * 512:(half + 1) * 512],
                                          ones_b[:],
                                          bias2[0:1, half * 512:(half + 1) * 512],
                                          start=True, stop=False)
                        if 0 <= u < T:
                            te.wait_ge(sm.s_hl, u + 1)
                            te.wait_ge(sm.s_xpte, min(u + 3, T))
                        pinst = None
                        for r_ in range(8):
                            if 0 <= u < T:
                                if r_ >= 2:
                                    te.wait_ge(sm.s_hte, 8 * u + r_ - 1)
                                inst = None
                                for j in range(4):
                                    col = (4 * r_ + j) * P
                                    inst = te.transpose(
                                        pqkv[:, (r_ % 2) * 512 + j * P:
                                             (r_ % 2) * 512 + (j + 1) * P],
                                        hf[:, u % NHF, col:col + P], idn_f[:])
                                inst.then_inc(sm.s_htr, 1)
                            if 0 <= w < T:
                                for c in range(4 * r_, 4 * r_ + 4):
                                    for half in range(2):
                                        pinst = te.matmul(
                                            pPJ[:, half * 512:(half + 1) * 512],
                                            hT[:, w % 2, c, :],
                                            wpq[:, c, half * 512:(half + 1) * 512],
                                            start=False, stop=(c == NWPP - 1))
                        if 0 <= w < T:
                            pinst.then_inc(sm.s_proj, 1)
                    if t < T:
                        te.wait_ge(sm.s_ga, t + 1)
                        inst = None
                        for c in range(8):
                            for piece in range(2):
                                te.matmul(
                                    pFA[:, piece * 512:(piece + 1) * 512],
                                    xpT[:, t % 3, c * P:(c + 1) * P],
                                    wfcq[:, c, 2048 + piece * 512:
                                         2048 + (piece + 1) * 512],
                                    start=(c == 0), stop=(c == 7))
                            for piece in range(2):
                                inst = te.matmul(
                                    pFB[:, piece * 512:(piece + 1) * 512],
                                    xpT[:, t % 3, c * P:(c + 1) * P],
                                    wfcq[:, c, 3072 + piece * 512:
                                         3072 + (piece + 1) * 512],
                                    start=(c == 0), stop=(c == 7))
                        inst.then_inc(sm.s_fcb, 1)
                    v2 = t + 2
                    if 2 <= v2 < T:
                        te.wait_ge(sxplL[v2 % 4], 16 * (v2 // 4 + 1))
                        if 0 <= u < T:
                            te.wait_ge(sm.s_hte, 8 * u + 8)
                        te.wait_ge(sm.s_xpte, t + 2)
                        inst = None
                        for c in range(8):
                            inst = te.transpose(pqkv[:, c * P:(c + 1) * P],
                                                xpb[:, v2 % NXP, c * P:(c + 1) * P],
                                                idn_f[:])
                        inst.then_inc(sm.s_xptr, 1)

    return nc


_NC_CACHE = [None]


def kernel(L0_attn_x, L0_attn_Wq, L0_attn_Wk, L0_attn_Wv, L0_attn_Wo,
           L0_attn_lambda_q1, L0_attn_lambda_k1, L0_attn_lambda_q2,
           L0_attn_lambda_k2, L0_attn_ln_w, L0_attn_ln_b,
           L0_ffn_W_fc, L0_ffn_ln_w, L0_ffn_ln_b, L0_ffn_W_proj):
    f32 = np.float32
    x = np.asarray(L0_attn_x, f32)
    Wq = np.asarray(L0_attn_Wq, f32)
    Wk = np.asarray(L0_attn_Wk, f32)
    Wv = np.asarray(L0_attn_Wv, f32)
    Wo = np.asarray(L0_attn_Wo, f32)

    def perm_qk(W):
        Wh = W.reshape(D, 16, HD)
        return np.ascontiguousarray(
            np.concatenate([Wh[:, 0::2, :], Wh[:, 1::2, :]], axis=1
                           ).reshape(D, D))

    wqkv = np.ascontiguousarray(
        np.concatenate([perm_qk(Wq), perm_qk(Wk), Wv], axis=1))
    wfct = np.ascontiguousarray(np.asarray(L0_ffn_W_fc, f32).T)
    wpt = np.ascontiguousarray(np.asarray(L0_ffn_W_proj, f32).T)
    lamv = np.ascontiguousarray(np.concatenate([
        np.asarray(L0_attn_lambda_q1, f32), np.asarray(L0_attn_lambda_k1, f32),
        np.asarray(L0_attn_lambda_q2, f32), np.asarray(L0_attn_lambda_k2, f32)])
        [None, :])
    alnw = np.ascontiguousarray(np.tile(np.asarray(L0_attn_ln_w, f32)[None, :],
                                        (P, 1)))
    alnb = np.ascontiguousarray(np.asarray(L0_attn_ln_b, f32)[:, None])
    flnw = np.ascontiguousarray(np.asarray(L0_ffn_ln_w, f32).reshape(NWPP, P).T)
    flnb = np.ascontiguousarray(np.asarray(L0_ffn_ln_b, f32).reshape(NWPP, P).T)

    pidx = np.arange(P)
    mask = ((pidx[:, None] // 8) == (pidx[None, :] // 8)).astype(f32)
    onesc = np.ones((1, P), f32)
    idnc = np.eye(P, dtype=f32)

    if _NC_CACHE[0] is None:
        _NC_CACHE[0] = build_program()
    nc = _NC_CACHE[0]

    in_maps = []
    for i in range(NCORES):
        in_maps.append(dict(
            x=np.ascontiguousarray(x[i]), wqkv=wqkv, wo=np.ascontiguousarray(Wo),
            wfct=wfct, wpt=wpt, lamv=lamv, alnw=alnw, alnb=alnb, flnw=flnw,
            flnb=flnb, maskc=mask, onesc=onesc, idnc=idnc))
    res = run_bass_kernel_spmd(nc, in_maps, core_ids=list(range(NCORES)))
    out = np.stack([res.results[i]["out"] for i in range(NCORES)], axis=0)
    return out.astype(np.float32)


# version 21
# speedup vs baseline: 1.0716x; 1.0716x over previous
import math
from contextlib import ExitStack

import numpy as np

import concourse.bass as bass
import concourse.mybir as mybir
from concourse.bass_utils import run_bass_kernel_spmd

F32 = mybir.dt.float32
BF16 = mybir.dt.bfloat16
F8 = mybir.dt.float8e4
I32 = mybir.dt.int32
AF = mybir.ActivationFunctionType
ALU = mybir.AluOpType
AX = mybir.AxisListType

NCORES = 8
S = 4096
D = 1024
FF = 4096
T = 32
P = 128
HD = 64
E2 = 128

LAMBDA_INIT = 0.8 - 0.6 * math.exp(-0.3 * 12)
SCALE = 1.0 / 64.0 ** 0.25
RMS_EPS = 1e-8
LN_EPS = 1e-5
AM_EPS = 1e-6
LN1MLI = math.log(1.0 - LAMBDA_INIT)

NFCP = 32
NWPP = 32
NPIECE = NFCP + NWPP


def mkap(t, offset, dims):
    ap = t[:]
    if ap.ndim > 2:
        names = [chr(97 + i) for i in range(ap.ndim - 1)]
        pat = "p " + " ".join(names) + " -> p (" + " ".join(names) + ")"
        ap = ap.rearrange(pat)
    while ap.ndim < len(dims):
        ap = ap.unsqueeze(ap.ndim)
    while ap.ndim > len(dims):
        ap = ap.squeeze(ap.ndim - 1)
    for i, (st, c) in enumerate(dims):
        ap.ap[i] = [st, c]
    ap.offset = offset
    return ap


def build_program():
    nc = bass.Bass("TRN2", target_bir_lowering=False)

    x_d = nc.dram_tensor("x", [S, D], F32, kind="ExternalInput").ap()
    wqkv_d = nc.dram_tensor("wqkv", [D, 3 * D], F32, kind="ExternalInput").ap()
    wo_d = nc.dram_tensor("wo", [D, D], F32, kind="ExternalInput").ap()
    wfct_d = nc.dram_tensor("wfct", [D, FF], F32, kind="ExternalInput").ap()
    wpt_d = nc.dram_tensor("wpt", [FF, D], F32, kind="ExternalInput").ap()
    lamv_d = nc.dram_tensor("lamv", [1, 4 * HD], F32, kind="ExternalInput").ap()
    alnw_d = nc.dram_tensor("alnw", [P, E2], F32, kind="ExternalInput").ap()
    alnb_d = nc.dram_tensor("alnb", [P, 1], F32, kind="ExternalInput").ap()
    flnw_d = nc.dram_tensor("flnw", [P, NWPP], F32, kind="ExternalInput").ap()
    flnb_d = nc.dram_tensor("flnb", [P, NWPP], F32, kind="ExternalInput").ap()
    mask_d = nc.dram_tensor("maskc", [P, P], F32, kind="ExternalInput").ap()
    ones_d = nc.dram_tensor("onesc", [1, P], F32, kind="ExternalInput").ap()
    idn_d = nc.dram_tensor("idnc", [P, P], F32, kind="ExternalInput").ap()

    out_d = nc.dram_tensor("out", [S, D], F32, kind="ExternalOutput").ap()

    xp_d = nc.dram_tensor("xp_s", [S, D], F32, kind="Internal").ap()
    wfcq_d = nc.dram_tensor("wfcq_s", [D, FF], F8, kind="Internal").ap()
    wpq_d = nc.dram_tensor("wpq_s", [FF, D], F8, kind="Internal").ap()
    bc_d = nc.dram_tensor("bc_s", [1, 512], F32, kind="Internal").ap()
    vd_d = nc.dram_tensor("vd_s", [2, P, D], BF16, kind="Internal").ap()
    wd_d = nc.dram_tensor("wd_s", [2, P, HD], BF16, kind="Internal").ap()

    with ExitStack() as ctx:
        en = ctx.enter_context

        ones_b = en(nc.sbuf_tensor("ones_b", [1, P], BF16))
        idn_f = en(nc.sbuf_tensor("idn_f", [P, P], F32))
        flnw = en(nc.sbuf_tensor("flnw_s", [P, NWPP], F32))
        flnbb = en(nc.sbuf_tensor("flnbb_s", [P, NWPP], BF16))
        bias2 = en(nc.sbuf_tensor("bias2_s", [1, D], BF16))
        CONSTS = [SCALE, 1.0 / E2, RMS_EPS, -0.5, LN1MLI, LN_EPS]
        cst = en(nc.sbuf_tensor("cst_s", [P, len(CONSTS)], F32))
        for i_, val_ in enumerate(CONSTS):
            nc.const_aps.register(F32, val_, cst[:, i_:i_ + 1])
        pqkv = en(nc.psum_tensor("pqkv", [P, 3072], F32))
        pmid = en(nc.psum_tensor("pmid", [P, 1024], F32))

        sems = {}
        for name in ["s_ld", "s_xte", "s_qkv", "s_qke", "s_ve", "s_ngx", "s_exp", "s_ngx2", "s_exp2", "s_wav", "s_S",
                     "s_av", "s_t1", "s_y", "s_ytr", "s_yte", "s_wo", "s_z1v",
                     "s_ln1v", "s_r1", "s_nm1", "s_xp", "s_wg", "s_thr", "s_q", "s_gm1", "s_gm2", "s_gm3", "s_thrv", "s_cst", "s_fs2",
                     "s_phA",
                     "s_fv", "s_lam1", "s_lam2", "s_lam3", "s_lam4", "s_lam5",
                     "s_lnb1", "s_b1m", "s_b1e",
                     "s_wql", "s_wqf", "s_b2m", "s_b2e",
                     "s_g2", "s_fg", "s_p2",
                     "s_xpte", "s_xptr", "s_fca", "s_fcb",
                     "s_ga", "s_gb", "s_hl",
                     "s_htr", "s_hte", "s_proj", "s_z3v", "s_nm3", "s_out", "s_outd"]:
            sems[name] = en(nc.semaphore(name))
        sm = type("SM", (), sems)
        def semlist(nm, k):
            return [en(nc.semaphore(f"{nm}{i}")) for i in range(k)]
        sxL = semlist("sxl", 4)
        svh1 = semlist("svh1", 2)
        svh2 = semlist("svh2", 2)
        swh1 = semlist("swh1", 2)
        swh2 = semlist("swh2", 2)
        swp1 = semlist("swp1", 4)
        swp2 = semlist("swp2", 4)
        sqdL = semlist("sqdl", 2)
        sxpsL = semlist("sxpsl", 2)
        sxplL = semlist("sxpll", 4)
        soutL = semlist("soutl", 2)
        fv = [0]

        block = en(nc.Block())

        with ExitStack() as actx:
            aen = actx.enter_context
            wqkv = aen(nc.sbuf_tensor("wqkv_s", [P, 8, 3 * D], BF16))
            wo = aen(nc.sbuf_tensor("wo_s", [P, 8, D], BF16))
            maskb = aen(nc.sbuf_tensor("maskb", [P, P], BF16))
            lamv = aen(nc.sbuf_tensor("lamv_s", [1, 4 * HD], F32))
            lsc = aen(nc.sbuf_tensor("lsc_s", [1, 4 * HD], F32))
            lamfin = aen(nc.sbuf_tensor("lamfin_s", [1, 1], F32))
            alnw = aen(nc.sbuf_tensor("alnw_s", [P, E2], F32))
            alnb = aen(nc.sbuf_tensor("alnb_s", [P, 1], F32))
            alnbp = aen(nc.sbuf_tensor("alnbp_s", [P, 1], BF16))
            neglam = aen(nc.sbuf_tensor("neglam_s", [P, 1], F32))
            bias1 = aen(nc.sbuf_tensor("bias1_s", [1, D], BF16))
            NXB = 4
            xb = aen(nc.sbuf_tensor("xb_s", [P, NXB, D], F32))
            xT = aen(nc.sbuf_tensor("xT_s", [P, 2, D], BF16))
            qsb = aen(nc.sbuf_tensor("qsb_s", [P, 2, D], BF16))
            ksb = aen(nc.sbuf_tensor("ksb_s", [P, 2, D], BF16))
            vsb = aen(nc.sbuf_tensor("vsb_s", [P, 2, D], BF16))
            prod = aen(nc.sbuf_tensor("prod_s", [P, 4096], BF16))
            prod2 = aen(nc.sbuf_tensor("prod2_s", [P, 4096], BF16))
            trg1 = aen(nc.sbuf_tensor("trg1_s", [P, 2048], BF16))
            trg2 = aen(nc.sbuf_tensor("trg2_s", [P, 1024], BF16))
            trg3 = aen(nc.sbuf_tensor("trg3_s", [P, 512], F32))
            trg4 = aen(nc.sbuf_tensor("trg4_s", [P, 256], F32))
            trg5 = aen(nc.sbuf_tensor("trg5_s", [P, 128], F32))
            scor = aen(nc.sbuf_tensor("scor_s", [P, 2, 64], F32))
            mxt = aen(nc.sbuf_tensor("mxt_s", [P, 2], F32))
            ngx = aen(nc.sbuf_tensor("ngx_s", [P, 2], F32))
            exw = aen(nc.sbuf_tensor("exw_s", [P, 2, 64], F32))
            zsw = aen(nc.sbuf_tensor("zsw_s", [P, 2, 8], F32))
            zrw = aen(nc.sbuf_tensor("zrw_s", [P, 2, 8], F32))
            aww = aen(nc.sbuf_tensor("aww_s", [P, 2, 64], F32))
            wav = aen(nc.sbuf_tensor("wav_s", [P, 2, HD], BF16))
            Bsc = aen(nc.sbuf_tensor("Bsc_s", [P, 2, HD], BF16))
            Vsc = aen(nc.sbuf_tensor("Vsc_s", [P, 2, D], BF16))
            Ssc = aen(nc.sbuf_tensor("Ssc_s", [P, 2, 8, P], BF16))
            msq = aen(nc.sbuf_tensor("msq_s", [P, 8], F32))
            sqsc = aen(nc.sbuf_tensor("sqsc_s", [P, E2], F32))
            rmsx = aen(nc.sbuf_tensor("rmsx_s", [P, 8], F32))
            t1f = aen(nc.sbuf_tensor("t1f_s", [P, 2, D], F32))
            yT = aen(nc.sbuf_tensor("yT_s", [P, 2, D], BF16))
            z1 = aen(nc.sbuf_tensor("z1_s", [P, 2, D], F32))
            bst = aen(nc.sbuf_tensor("bst_s", [P, 2, 6], F32))
            bag = aen(nc.sbuf_tensor("bag_s", [P, 2, 2], F32))
            rstd = aen(nc.sbuf_tensor("rstd_s", [P, 1], F32))
            nmu = aen(nc.sbuf_tensor("nmu_s", [P, 1], F32))
            xps = aen(nc.sbuf_tensor("xps_s", [P, 2, D], F32))
            wsb = aen(nc.sbuf_tensor("wsb_s", [P, 4, 1024], F32))
            posq = aen(nc.sbuf_tensor("posq_s", [P, 1024], F32))
            redq = aen(nc.sbuf_tensor("redq_s", [P, 1024], F32))
            qout = aen(nc.sbuf_tensor("qout_s", [P, 2, 1024], F8))
            accf = aen(nc.sbuf_tensor("accf_s", [P, 1], F32))
            accp = aen(nc.sbuf_tensor("accp_s", [P, 1], F32))
            redt = aen(nc.sbuf_tensor("redt_s", [P, 1], F32))
            gamrow = aen(nc.sbuf_tensor("gamrow_s", [1, 256], F32))
            thr4row = aen(nc.sbuf_tensor("thr4row_s", [1, 4], F32))
            thr4 = aen(nc.sbuf_tensor("thr4_s", [P, 4], F32))

            P1T = 0
            GAMT = max(1, T // 4)
            P1N = -(-NPIECE // GAMT)
            P2T = GAMT + 1
            P2N = -(-NPIECE // (T + 3 - P2T))

            def _wsrc(pi, dram2):
                if pi < NFCP:
                    return dram2[0].rearrange("(c p) (h q) -> p c h q",
                                              p=P, q=1024)[:, pi // 4, pi % 4, :]
                return dram2[1].rearrange("(c p) q -> p c q", p=P)[:, pi - NFCP, :]

            def _p1_load(sy, pi):
                if pi >= 4:
                    sy.wait_ge(sm.s_wg, pi - 3)
                sy.dma_start(wsb[:, pi % 4, :], _wsrc(pi, (wfct_d, wpt_d))
                             ).then_inc(swp1[pi % 4], 16)

            def _p2_load(sy, pi):
                if pi == 0:
                    sy.wait_ge(sm.s_thr, 16)
                if pi >= 4:
                    sy.wait_ge(sm.s_q, pi - 3)
                sy.dma_start(wsb[:, pi % 4, :], _wsrc(pi, (wfct_d, wpt_d))
                             ).then_inc(swp2[pi % 4], 16)

            def _p2_store(sy, pi):
                sy.wait_ge(sm.s_q, pi + 1)
                sy.dma_start(_wsrc(pi, (wfcq_d, wpq_d)), qout[:, pi % 2, :]
                             ).then_inc(sqdL[pi % 2], 16)

            @block.gpsimd
            def _(g):
                g.dma_start(wqkv[:], wqkv_d.rearrange("(c p) n -> p c n", p=P)
                            ).then_inc(sm.s_ld, 16)
                g.dma_start(wo[:], wo_d.rearrange("(c p) n -> p c n", p=P)
                            ).then_inc(sm.s_ld, 16)
                g.dma_start(maskb[:], mask_d).then_inc(sm.s_ld, 16)
                g.dma_start(ones_b[:], ones_d).then_inc(sm.s_ld, 16)
                g.dma_start(idn_f[:], idn_d).then_inc(sm.s_ld, 16)
                g.dma_start(lamv[:], lamv_d).then_inc(sm.s_ld, 16)
                g.dma_start(alnw[:], alnw_d).then_inc(sm.s_ld, 16)
                g.dma_start(alnb[:], alnb_d).then_inc(sm.s_ld, 16)
                g.dma_start(flnw[:], flnw_d).then_inc(sm.s_ld, 16)
                g.dma_start(flnbb[:], flnb_d).then_inc(sm.s_ld, 16)
                g.wait_ge(sm.s_ld, 16 * 10)
                for i_, val_ in enumerate(CONSTS):
                    g.memset(cst[:, i_:i_ + 1], val_)
                g.engine_nop().then_inc(sm.s_cst, 1)
                fg2 = [0]

                def gfence(inst):
                    inst.then_inc(sm.s_fg, 1)
                    fg2[0] += 1
                    g.wait_ge(sm.s_fg, fg2[0])

                for t in range(T + 2):
                    if P1T <= t < GAMT:
                        for j in range(P1N):
                            pi = (t - P1T) * P1N + j
                            if pi < NPIECE:
                                _p1_load(g, pi)
                    w_ = t - 1
                    if 0 <= w_ < T:
                        g.wait_ge(swh2[w_ % 2], 16 * (w_ // 2 + 1))
                        if w_ >= 2:
                            g.wait_ge(sm.s_av, w_ - 1)
                        inst = None
                        for gq in range(8):
                            in0 = mkap(Bsc, (w_ % 2) * HD + gq * 8,
                                       [(2 * HD, P), (0, 16), (1, 8)])
                            in1 = mkap(maskb, 0, [(P, P), (8, 16), (1, 8)])
                            outp = mkap(Ssc, (w_ % 2) * 8 * P + gq * P,
                                        [(2 * 8 * P, P), (8, 16), (1, 8)])
                            inst = g.tensor_tensor(outp, in0, in1, ALU.mult)
                        inst.then_inc(sm.s_S, 1)
                    uy = t - 2
                    if 0 <= uy < T:
                        g.wait_ge(sm.s_t1, uy + 1)
                        t1v = mkap(t1f, (uy % 2) * D,
                                   [(2 * D, P), (P, 8), (1, P)])
                        gfence(g.tensor_tensor(
                            t1v, t1v,
                            mkap(alnw, 0, [(P, P), (0, 8), (1, P)]),
                            ALU.mult))
                        g.tensor_tensor(
                            t1v, t1v,
                            mkap(rmsx, 0, [(8, P), (1, 8), (0, P)]),
                            ALU.mult).then_inc(sm.s_y, 1)
                    if t < T:
                        g.wait_ge(sm.s_p2, t + 1)
                        if t >= 1:
                            g.wait_ge(sm.s_exp2, t)
                        gfence(g.tensor_tensor(
                            mkap(trg1, 0, [(2048, P), (32, 64), (1, 32)]),
                            mkap(prod2, 0, [(4096, P), (64, 64), (1, 32)]),
                            mkap(prod2, 32, [(4096, P), (64, 64), (1, 32)]),
                            ALU.add))
                        plan = [(trg1, 32, trg2), (trg2, 16, trg3),
                                (trg3, 8, trg4), (trg4, 4, trg5), (trg5, 2, None)]
                        for cur, d, nxt in plan:
                            h_ = d // 2
                            cp = cur[:].ap[0][0]
                            if nxt is None:
                                outp = mkap(scor, 64, [(128, P), (1, 64), (1, 1)])
                            else:
                                npp = nxt[:].ap[0][0]
                                outp = mkap(nxt, 0, [(npp, P), (h_, 64), (1, h_)])
                            gfence(g.tensor_tensor(
                                outp,
                                mkap(cur, 0, [(cp, P), (d, 64), (1, h_)]),
                                mkap(cur, h_, [(cp, P), (d, 64), (1, h_)]),
                                ALU.add))
                        g.engine_nop().then_inc(sm.s_g2, 1)
                    if t >= P2T - 1:
                        for j in range(P2N):
                            pi = (t - (P2T - 1)) * P2N + j
                            if pi < NPIECE:
                                _p2_load(g, pi)
                g.engine_nop().then_inc(sm.s_phA, 1)

            @block.sync
            def _(sy):
                sy.wait_ge(sm.s_lam3, 1)
                sy.dma_start(mkap(bc_d.rearrange("a b -> (a b)"), 300,
                                  [(1, 1), (1, 1)]), lamfin[:]).then_inc(sm.s_lam4, 16)
                sy.wait_ge(sm.s_lam4, 16)
                sy.dma_start(neglam[:],
                             mkap(bc_d.rearrange("a b -> (a b)"), 300,
                                  [(0, P), (1, 1)])).then_inc(sm.s_lam5, 16)
                for t in range(T + 4):
                    if t == GAMT:
                        sy.wait_ge(sm.s_wg, NPIECE)
                        sy.dma_start(mkap(bc_d.rearrange("a b -> (a b)"), 0,
                                          [(1, P), (1, 1)]), accf[:]
                                     ).then_inc(sm.s_gm1, 16)
                        sy.dma_start(mkap(bc_d.rearrange("a b -> (a b)"), P,
                                          [(1, P), (1, 1)]), accp[:]
                                     ).then_inc(sm.s_gm1, 16)
                        sy.wait_ge(sm.s_gm1, 32)
                        sy.dma_start(gamrow[:], bc_d[0:1, 0:256]
                                     ).then_inc(sm.s_gm2, 16)
                        sy.wait_ge(sm.s_thrv, 1)
                        sy.dma_start(mkap(bc_d.rearrange("a b -> (a b)"), 256,
                                          [(1, 1), (1, 4)]), thr4row[:]
                                     ).then_inc(sm.s_gm3, 16)
                        sy.wait_ge(sm.s_gm3, 16)
                        sy.dma_start(thr4[:],
                                     mkap(bc_d.rearrange("a b -> (a b)"), 256,
                                          [(0, P), (1, 4)])).then_inc(sm.s_thr, 16)
                    if t >= P2T - 1:
                        for j in range(P2N):
                            pi = (t - (P2T - 1)) * P2N + j
                            if 0 <= pi - 2 < NPIECE - 2:
                                _p2_store(sy, pi - 2)
                            if pi == NPIECE - 1:
                                _p2_store(sy, NPIECE - 2)
                                _p2_store(sy, NPIECE - 1)
                    if t < T:
                        sy.wait_ge(sm.s_ve, t + 1)
                        if t >= 2:
                            sy.wait_ge(svh2[t % 2], 16 * (t // 2))
                        sy.dma_start(vd_d[t % 2, :, :], vsb[:, t % 2, :]
                                     ).then_inc(svh1[t % 2], 16)
                        sy.wait_ge(svh1[t % 2], 16 * (t // 2 + 1))
                        if t >= 2:
                            sy.wait_ge(sm.s_av, t - 1)
                        vsrc = mkap(vd_d.rearrange("a b c -> (a b c)"),
                                    (t % 2) * P * D, [(128, P), (16384, 8), (1, E2)])
                        sy.dma_start(Vsc[:, t % 2, :], vsrc).then_inc(svh2[t % 2], 16)
                sy.wait_ge(svh2[0], 16 * ((T + 1) // 2))
                sy.wait_ge(svh2[1], 16 * (T // 2)).then_inc(sm.s_phA, 1)

            fs2 = [0]

            @block.scalar
            def _(s):
                def sfence(inst):
                    inst.then_inc(sm.s_fs2, 1)
                    fs2[0] += 1
                    s.wait_ge(sm.s_fs2, fs2[0])
                s.wait_ge(sm.s_cst, 1)
                for j in range(2):
                    s.dma_start(xb[:, j % NXB, :], x_d[j * P:(j + 1) * P, :]
                                ).then_inc(sxL[j % 4], 16)
                s.wait_ge(sm.s_lam1, 1)
                s.activation(lsc[0:1, 130:131], lsc[0:1, 128:129], AF.Exp)
                s.activation(lsc[0:1, 131:132], lsc[0:1, 129:130], AF.Exp
                             ).then_inc(sm.s_lam2, 1)
                s.wait_ge(sm.s_b1m, 1)
                s.activation(bias1[:], pmid[0:1, 0:1024], AF.Copy).then_inc(sm.s_b1e, 1)
                s.wait_ge(sm.s_xptr, 1)
                s.activation(xT[:, 0, :], pmid[:], AF.Copy).then_inc(sm.s_xte, 1)

                for t in range(T + 4):
                    u = t - 2
                    if 0 <= u < T:
                        s.wait_ge(sm.s_av, u + 1)
                        if u >= 2:
                            s.wait_ge(sm.s_ytr, u - 1)
                        s.activation(t1f[:, u % 2, :], pmid[:], AF.Copy)
                        for gq in range(8):
                            sfence(s.activation(sqsc[:], pmid[:, gq * P:(gq + 1) * P],
                                                AF.Square,
                                                accum_out=msq[:, gq:gq + 1]))
                        sfence(s.activation(rmsx[:], msq[:], AF.Ln, scale=1.0 / E2,
                                            bias=RMS_EPS))
                        s.activation(rmsx[:], rmsx[:], AF.Exp, scale=-0.5,
                                     bias=LN1MLI).then_inc(sm.s_t1, 1)
                    if t < T:
                        s.wait_ge(sm.s_qkv, 2 * t + 1)
                        if t >= 2:
                            s.wait_ge(sm.s_ngx, t - 1)
                            s.wait_ge(sm.s_g2, t - 1)
                            s.wait_ge(svh1[t % 2], 16 * (t // 2))
                        s.activation(qsb[:, t % 2, :], pqkv[:, 0:1024], AF.Copy)
                        s.activation(ksb[:, t % 2, :], pqkv[:, 1024:2048], AF.Copy
                                     ).then_inc(sm.s_qke, 1)
                    u3 = t - 3
                    if 0 <= u3 < T:
                        s.wait_ge(sm.s_ln1v, u3 + 1)
                        sfence(s.activation(rstd[:], bag[:, u3 % 2, 1:2], AF.Ln,
                                            bias=LN_EPS))
                        s.activation(rstd[:], rstd[:], AF.Exp, scale=-0.5
                                     ).then_inc(sm.s_r1, 1)
                        s.wait_ge(sm.s_nm1, u3 + 1)
                        if u3 >= 2:
                            s.wait_ge(sxpsL[u3 % 2], 16 * (u3 // 2))
                        s.activation(xps[:, u3 % 2, :], z1[:, u3 % 2, :], AF.Identity,
                                     scale=rstd[:], bias=nmu[:]).then_inc(sm.s_xp, 1)
                        s.wait_ge(sm.s_xp, u3 + 1)
                        s.dma_start(xp_d[u3 * P:(u3 + 1) * P, :], xps[:, u3 % 2, :]
                                    ).then_inc(sxpsL[u3 % 2], 16)
                    if t < T:
                        s.wait_ge(sm.s_qkv, 2 * t + 2)
                        s.activation(vsb[:, t % 2, :], pqkv[:, 2048:3072], AF.Copy
                                     ).then_inc(sm.s_ve, 1)
                    if 0 <= u < T:
                        s.wait_ge(sm.s_ytr, u + 1)
                        if u >= 2:
                            s.wait_ge(sm.s_wo, u - 1)
                        s.activation(yT[:, u % 2, :], pmid[:], AF.Copy
                                     ).then_inc(sm.s_yte, 1)
                    if t < T:
                        s.wait_ge(sm.s_ngx, t + 1)
                        s.activation(exw[:, 0, :], scor[:, 0, :],
                                     AF.Exp, scale=SCALE, bias=ngx[:, 0:1]
                                     ).then_inc(sm.s_exp, 1)
                    v_ = t + 1
                    if 1 <= v_ < T:
                        s.wait_ge(sm.s_xptr, v_ + 1)
                        if v_ >= 2:
                            s.wait_ge(sm.s_qkv, 2 * (v_ - 1))
                        s.activation(xT[:, v_ % 2, :], pmid[:], AF.Copy
                                     ).then_inc(sm.s_xte, 1)
                    if t + 2 < T:
                        if t + 2 >= NXB:
                            s.wait_ge(sm.s_z1v, t + 2 - NXB + 1)
                        s.dma_start(xb[:, (t + 2) % NXB, :],
                                    x_d[(t + 2) * P:(t + 3) * P, :]
                                    ).then_inc(sxL[(t + 2) % 4], 16)
                    if t < T:
                        s.wait_ge(sm.s_ngx2, t + 1)
                        s.activation(exw[:, 1, :], scor[:, 1, :],
                                     AF.Exp, scale=SCALE, bias=ngx[:, 1:2]
                                     ).then_inc(sm.s_exp2, 1)
                        s.wait_ge(sm.s_wav, t + 1)
                        if t >= 2:
                            s.wait_ge(swh2[t % 2], 16 * (t // 2))
                        s.dma_start(wd_d[t % 2, :, :], wav[:, t % 2, :]
                                    ).then_inc(swh1[t % 2], 16)
                        s.wait_ge(swh1[t % 2], 16 * (t // 2 + 1))
                        if t >= 2:
                            s.wait_ge(sm.s_S, t - 1)
                        wsrc = mkap(wd_d.rearrange("a b c -> (a b c)"),
                                    (t % 2) * P * HD, [(8, P), (1024, 8), (1, 8)])
                        s.dma_start(Bsc[:, t % 2, :], wsrc).then_inc(swh2[t % 2], 16)
                if True:
                    s.wait_ge(sxpsL[0], 16 * ((T + 1) // 2))
                    s.wait_ge(sxpsL[1], 16 * (T // 2)).then_inc(sm.s_phA, 1)

            @block.vector
            def _(v):
                def fence(inst):
                    inst.then_inc(sm.s_fv, 1)
                    fv[0] += 1
                    v.wait_ge(sm.s_fv, fv[0])

                v.wait_ge(sm.s_ld, 16 * 10)
                v.tensor_tensor(lsc[0:1, 0:64], lamv[0:1, 0:64],
                                lamv[0:1, 64:128], ALU.mult)
                fence(v.tensor_tensor(lsc[0:1, 64:128], lamv[0:1, 128:192],
                                      lamv[0:1, 192:256], ALU.mult))
                v.reduce_sum(lsc[0:1, 128:129], lsc[0:1, 0:64], AX.X)
                fence(v.reduce_sum(lsc[0:1, 129:130], lsc[0:1, 64:128], AX.X))
                v.engine_nop().then_inc(sm.s_lam1, 1)
                v.wait_ge(sm.s_lam2, 1)
                fence(v.tensor_sub(lsc[0:1, 132:133], lsc[0:1, 131:132],
                                   lsc[0:1, 130:131]))
                fence(v.tensor_scalar(lamfin[:], lsc[0:1, 132:133], -LAMBDA_INIT,
                                      None, ALU.add))
                v.engine_nop().then_inc(sm.s_lam3, 1)
                v.memset(accf[:], 0.0)
                v.memset(accp[:], 0.0)
                v.tensor_scalar(alnbp[:], alnb[:], 1.0 - LAMBDA_INIT, None,
                                ALU.mult).then_inc(sm.s_lnb1, 1)

                for t in range(T + 4):
                    if P1T <= t < GAMT:
                        for j in range(P1N):
                            pi = (t - P1T) * P1N + j
                            if pi < NPIECE:
                                v.wait_ge(swp1[pi % 4], 16 * (pi // 4 + 1))
                                acc = accf if pi < NFCP else accp
                                fence(v.reduce_sum(redt[:], wsb[:, pi % 4, :], AX.X,
                                                   apply_absolute_value=True))
                                fence(v.tensor_tensor(acc[:], acc[:], redt[:],
                                                      ALU.add))
                                v.engine_nop().then_inc(sm.s_wg, 1)
                    if t >= P2T:
                        for j in range(P2N):
                            pi = (t - P2T) * P2N + j
                            if pi < NPIECE:
                                v.wait_ge(swp2[pi % 4], 16 * (pi // 4 + 1))
                                co = 0 if pi < NFCP else 2
                                if pi == 0:
                                    v.wait_ge(sm.s_thr, 16)
                                if pi >= 2:
                                    v.wait_ge(sqdL[pi % 2], 16 * (pi // 2))
                                v.tensor_scalar(posq[:], wsb[:, pi % 4, :],
                                                thr4[:, co:co + 1], None, ALU.is_gt)
                                fence(v.tensor_scalar(redq[:], wsb[:, pi % 4, :],
                                                      thr4[:, co + 1:co + 2], -1.0,
                                                      ALU.is_gt, ALU.add))
                                fence(v.tensor_tensor(qout[:, pi % 2, :], posq[:],
                                                      redq[:], ALU.add))
                                v.engine_nop().then_inc(sm.s_q, 1)
                    if t == GAMT:
                        v.wait_ge(sm.s_gm2, 16)
                        fence(v.reduce_sum(thr4row[0:1, 0:1], gamrow[0:1, 0:P],
                                           AX.X))
                        fence(v.reduce_sum(thr4row[0:1, 2:3], gamrow[0:1, P:2 * P],
                                           AX.X))
                        fence(v.tensor_scalar(thr4row[0:1, 0:1], thr4row[0:1, 0:1],
                                              0.5 / (D * FF), 0.5 * AM_EPS,
                                              ALU.mult, ALU.add))
                        fence(v.tensor_scalar(thr4row[0:1, 2:3], thr4row[0:1, 2:3],
                                              0.5 / (D * FF), 0.5 * AM_EPS,
                                              ALU.mult, ALU.add))
                        fence(v.tensor_scalar(thr4row[0:1, 1:2], thr4row[0:1, 0:1],
                                              -1.0, None, ALU.mult))
                        fence(v.tensor_scalar(thr4row[0:1, 3:4], thr4row[0:1, 2:3],
                                              -1.0, None, ALU.mult))
                        v.engine_nop().then_inc(sm.s_thrv, 1)
                    u3v = t - 3
                    if 0 <= u3v < T:
                        v.wait_ge(sm.s_r1, u3v + 1)
                        if u3v >= 1:
                            v.wait_ge(sm.s_xp, u3v)
                        v.scalar_tensor_tensor(nmu[:], bag[:, u3v % 2, 0:1], -1.0,
                                               rstd[:], ALU.mult, ALU.mult
                                               ).then_inc(sm.s_nm1, 1)
                    uy = t - 2
                    if t < T:
                        v.wait_ge(sm.s_qke, t + 1)
                        if t >= 1:
                            v.wait_ge(sm.s_g2, t)
                        out4b = mkap(prod2, 0,
                                     [(4096, P), (512, 8), (64, 8), (1, 64)])
                        in0b = mkap(qsb, (t % 2) * D + 512,
                                    [(2 * D, P), (0, 8), (64, 8), (1, 64)])
                        in1b = mkap(ksb, (t % 2) * D + 512,
                                    [(2 * D, P), (64, 8), (0, 8), (1, 64)])
                        v.tensor_tensor(out4b, in0b, in1b, ALU.mult
                                        ).then_inc(sm.s_p2, 1)
                        out4 = mkap(prod, 0,
                                    [(4096, P), (512, 8), (64, 8), (1, 64)])
                        in0 = mkap(qsb, (t % 2) * D,
                                   [(2 * D, P), (0, 8), (64, 8), (1, 64)])
                        in1 = mkap(ksb, (t % 2) * D,
                                   [(2 * D, P), (64, 8), (0, 8), (1, 64)])
                        fence(v.tensor_tensor(out4, in0, in1, ALU.mult))
                        fence(v.tensor_reduce(
                            scor[:, 0, :],
                            mkap(prod, 0, [(4096, P), (64, 64), (1, 64)]),
                            AX.X, ALU.add))
                        fence(v.tensor_reduce(mxt[:, 0:1], scor[:, 0, :],
                                              AX.X, ALU.max))
                        v.tensor_scalar(ngx[:, 0:1], mxt[:, 0:1], -SCALE, None,
                                        ALU.mult).then_inc(sm.s_ngx, 1)
                    if 0 <= uy < T:
                        v.wait_ge(sm.s_wo, uy + 1)
                        v.wait_ge(sxL[uy % 4], 16 * (uy // 4 + 1))
                        if uy >= 2:
                            v.wait_ge(sm.s_xp, uy - 1)
                        fence(v.tensor_tensor(z1[:, uy % 2, :], pqkv[:, 0:1024],
                                              xb[:, uy % NXB, :], ALU.add))
                        v.engine_nop().then_inc(sm.s_z1v, 1)
                        if uy >= 2:
                            v.wait_ge(sm.s_r1, uy - 1)
                        v.bn_stats(bst[:, 0, :], z1[:, uy % 2, 0:512])
                        fence(v.bn_stats(bst[:, 1, :], z1[:, uy % 2, 512:1024]))
                        v.bn_aggr(bag[:, uy % 2, :], bst[:]).then_inc(sm.s_ln1v, 1)
                    if t < T:
                        v.wait_ge(sm.s_exp, t + 1)
                        exi1 = mkap(exw, 0, [(128, P), (1, 8), (8, 8)])
                        fence(v.tensor_reduce(zsw[:, 0, :], exi1, AX.X, ALU.add))
                        fence(v.reciprocal(zrw[:, 0, :], zsw[:, 0, :]))
                        fence(v.tensor_tensor(
                            mkap(aww, 0, [(128, P), (8, 8), (1, 8)]),
                            mkap(exw, 0, [(128, P), (8, 8), (1, 8)]),
                            mkap(zrw, 0, [(16, P), (0, 8), (1, 8)]), ALU.mult))
                        v.wait_ge(sm.s_g2, t + 1)
                        fence(v.tensor_reduce(mxt[:, 1:2], scor[:, 1, :],
                                              AX.X, ALU.max))
                        v.tensor_scalar(ngx[:, 1:2], mxt[:, 1:2], -SCALE, None,
                                        ALU.mult).then_inc(sm.s_ngx2, 1)
                        v.wait_ge(sm.s_exp2, t + 1)
                        exi2 = mkap(exw, 64, [(128, P), (1, 8), (8, 8)])
                        fence(v.tensor_reduce(zsw[:, 1, :], exi2, AX.X, ALU.add))
                        fence(v.reciprocal(zrw[:, 1, :], zsw[:, 1, :]))
                        fence(v.tensor_tensor(
                            mkap(aww, 64, [(128, P), (8, 8), (1, 8)]),
                            mkap(exw, 64, [(128, P), (8, 8), (1, 8)]),
                            mkap(zrw, 8, [(16, P), (0, 8), (1, 8)]), ALU.mult))
                        if t == 0:
                            v.wait_ge(sm.s_lam5, 16)
                        if t >= 2:
                            v.wait_ge(swh1[t % 2], 16 * (t // 2))
                        v.scalar_tensor_tensor(wav[:, t % 2, :], aww[:, 1, :],
                                               neglam[:], aww[:, 0, :],
                                               ALU.mult, ALU.add).then_inc(sm.s_wav, 1)

                v.engine_nop().then_inc(sm.s_phA, 1)

            @block.tensor
            def _(te):
                te.wait_ge(sm.s_ld, 16 * 10)
                te.wait_ge(sm.s_lnb1, 1)
                te.wait_ge(sm.s_lam5, 1)
                inst = None
                for half in range(2):
                    for c in range(8):
                        inst = te.matmul(pmid[0:1, half * 512:(half + 1) * 512],
                                         alnbp[:],
                                         wo[:, c, half * 512:(half + 1) * 512],
                                         start=(c == 0), stop=(c == 7))
                inst.then_inc(sm.s_b1m, 1)
                te.wait_ge(sm.s_b1e, 1)
                te.wait_ge(sxL[0], 16)
                inst = None
                for c in range(8):
                    inst = te.transpose(pmid[:, c * P:(c + 1) * P],
                                        xb[:, 0, c * P:(c + 1) * P], idn_f[:])
                inst.then_inc(sm.s_xptr, 1)

                for t in range(T + 4):
                    u = t - 2
                    if 0 <= u < T:
                        te.wait_ge(sm.s_S, u + 1)
                        te.wait_ge(svh2[u % 2], 16 * (u // 2 + 1))
                        te.wait_ge(sm.s_xte, min(t + 1, T))
                        inst = None
                        for gq in range(8):
                            inst = te.matmul(pmid[:, gq * P:(gq + 1) * P],
                                             Ssc[:, u % 2, gq, :],
                                             Vsc[:, u % 2, gq * P:(gq + 1) * P],
                                             start=True, stop=True)
                        inst.then_inc(sm.s_av, 1)
                    if t < T:
                        te.wait_ge(sm.s_xte, t + 1)
                        if t >= 1:
                            te.wait_ge(sm.s_ve, t)
                        if t >= 3:
                            te.wait_ge(sm.s_z1v, t - 2)
                        inst = None
                        for c in range(8):
                            for piece in range(6):
                                inst = te.matmul(
                                    pqkv[:, piece * 512:(piece + 1) * 512],
                                    xT[:, t % 2, c * P:(c + 1) * P],
                                    wqkv[:, c, piece * 512:(piece + 1) * 512],
                                    start=(c == 0), stop=(c == 7))
                                if c == 7 and piece == 3:
                                    inst.then_inc(sm.s_qkv, 1)
                        inst.then_inc(sm.s_qkv, 1)
                    if 0 <= u < T:
                        te.wait_ge(sm.s_y, u + 1)
                        inst = None
                        for c in range(8):
                            inst = te.transpose(pmid[:, c * P:(c + 1) * P],
                                                t1f[:, u % 2, c * P:(c + 1) * P],
                                                idn_f[:])
                        inst.then_inc(sm.s_ytr, 1)
                        te.wait_ge(sm.s_yte, u + 1)
                        te.wait_ge(sm.s_qke, min(t + 1, T))
                        inst = None
                        for half in range(2):
                            te.matmul(pqkv[:, half * 512:(half + 1) * 512], ones_b[:],
                                      bias1[0:1, half * 512:(half + 1) * 512],
                                      start=True, stop=False)
                            for n in range(8):
                                lhs = mkap(yT, (u % 2) * D + n,
                                           [(2 * D, P), (E2, 8), (8, 16)])
                                inst = te.matmul(
                                    pqkv[:, half * 512:(half + 1) * 512], lhs,
                                    wo[:, n, half * 512:(half + 1) * 512],
                                    start=False, stop=(n == 7))
                        inst.then_inc(sm.s_wo, 1)
                    v_ = t + 1
                    if 1 <= v_ < T:
                        te.wait_ge(sxL[v_ % 4], 16 * (v_ // 4 + 1))
                        te.wait_ge(sm.s_xte, min(t + 1, T))
                        if 0 <= u < T:
                            te.wait_ge(sm.s_yte, u + 1)
                        inst = None
                        for c in range(8):
                            inst = te.transpose(pmid[:, c * P:(c + 1) * P],
                                                xb[:, v_ % NXB, c * P:(c + 1) * P],
                                                idn_f[:])
                        inst.then_inc(sm.s_xptr, 1)

                te.wait_ge(sm.s_qkv, 2 * T).then_inc(sm.s_phA, 1)

        with ExitStack() as bctx:
            ben = bctx.enter_context
            wfcq = ben(nc.sbuf_tensor("wfcq_r", [P, 8, FF], F8))
            wpq = ben(nc.sbuf_tensor("wpq_r", [P, NWPP, D], F8))
            NXP = 4
            NHF = 3
            xpb = ben(nc.sbuf_tensor("xpb_s", [P, NXP, D], F32))
            xpT = ben(nc.sbuf_tensor("xpT_s", [P, 3, D], BF16))
            hf = ben(nc.sbuf_tensor("hf_s", [P, NHF, FF], F32))
            hT = ben(nc.sbuf_tensor("hT_s", [P, 2, NWPP, P], BF16))
            z3 = ben(nc.sbuf_tensor("z3_s", [P, 2, D], F32))
            bst2 = ben(nc.sbuf_tensor("bst2_s", [P, 8, 6], F32))
            bag2 = ben(nc.sbuf_tensor("bag2_s", [P, 2], F32))
            r2 = ben(nc.sbuf_tensor("r2_s", [P, 1], F32))
            nm2 = ben(nc.sbuf_tensor("nm2_s", [P, 1], F32))
            bst3 = ben(nc.sbuf_tensor("bst3_s", [P, 2, 6], F32))
            bag3 = ben(nc.sbuf_tensor("bag3_s", [P, 2], F32))
            r3 = ben(nc.sbuf_tensor("r3_s", [P, 2], F32))
            nm3 = ben(nc.sbuf_tensor("nm3_s", [P, 2], F32))
            outb = ben(nc.sbuf_tensor("outb_s", [P, 2, D], F32))
            nwi = ben(nc.sbuf_tensor("nwi_s", [P, 1], I32))
            nwy = ben(nc.sbuf_tensor("nwy_s", [P, 1], F32))
            nwv = ben(nc.sbuf_tensor("nwv_s", [P, 1], F32))
            nwh = ben(nc.sbuf_tensor("nwh_s", [P, 1], F32))
            nwa = ben(nc.sbuf_tensor("nwa_s", [P, 1], F32))
            nwb = ben(nc.sbuf_tensor("nwb_s", [P, 1], F32))

            pHT = pqkv[:, 0:1024]
            pPJ = pqkv[:, 1024:2048]
            pFA = pqkv[:, 2048:3072]
            pFB = pmid[:, 0:1024]

            @block.sync
            def _(sy):
                sy.wait_ge(sm.s_phA, 5)
                sy.wait_ge(sqdL[0], 16 * (NPIECE // 2))
                sy.wait_ge(sqdL[1], 16 * (NPIECE // 2))
                for c in range(8):
                    sy.dma_start(wfcq[:, c, :],
                                 wfcq_d.rearrange("(c p) n -> p c n", p=P)[:, c, :]
                                 ).then_inc(sm.s_wqf, 16)
                for c in range(NWPP):
                    sy.dma_start(wpq[:, c, :],
                                 wpq_d.rearrange("(c p) n -> p c n", p=P)[:, c, :]
                                 ).then_inc(sm.s_wql, 16)
                for t in range(T):
                    sy.wait_ge(sxpsL[t % 2], 16 * (t // 2 + 1))
                    if t >= 4:
                        sy.wait_ge(sm.s_z3v, t - 3)
                    sy.dma_start(xpb[:, t % NXP, :], xp_d[t * P:(t + 1) * P, :]
                                 ).then_inc(sxplL[t % 4], 16)
                sy.wait_ge(soutL[0], 16 * ((T + 1) // 2))
                sy.wait_ge(soutL[1], 16 * (T // 2))

            @block.gpsimd
            def _(g):
                g.wait_ge(sm.s_phA, 5)
                for w3 in range(T):
                    g.wait_ge(sm.s_out, w3 + 1)
                    g.dma_start(out_d[w3 * P:(w3 + 1) * P, :], outb[:, w3 % 2, :]
                                ).then_inc(soutL[w3 % 2], 16)

            @block.scalar
            def _(s):
                s.wait_ge(sm.s_phA, 5)
                s.wait_ge(sm.s_xptr, T + 1)
                s.activation(xpT[:, 0, :], pHT, AF.Copy).then_inc(sm.s_xpte, 1)
                s.wait_ge(sm.s_xptr, T + 2)
                s.activation(xpT[:, 1, :], pHT, AF.Copy).then_inc(sm.s_xpte, 1)
                for t in range(T + 5):
                    u = t - 1
                    if t < T:
                        s.wait_ge(sm.s_fca, t + 1)
                        if t >= NHF:
                            s.wait_ge(sm.s_htr, 8 * (t - NHF) + 8)
                        s.activation(hf[:, t % NHF, 0:1024], pFA, AF.Gelu)
                        s.activation(hf[:, t % NHF, 1024:2048], pFB, AF.Gelu
                                     ).then_inc(sm.s_ga, 1)
                    if t == 1:
                        s.wait_ge(sm.s_b2m, NWPP)
                        s.activation(bias2[:], pPJ[0:1, :], AF.Copy
                                     ).then_inc(sm.s_b2e, 1)
                    if 0 <= u < T:
                        if u >= 2:
                            s.wait_ge(sm.s_proj, u - 1)
                        for r_ in range(8):
                            s.wait_ge(sm.s_htr, 8 * u + r_ + 1)
                            inst = None
                            for j in range(4):
                                c = 4 * r_ + j
                                inst = s.activation(
                                    hT[:, u % 2, c, :],
                                    pqkv[:, (r_ % 2) * 512 + j * P:
                                         (r_ % 2) * 512 + (j + 1) * P],
                                    AF.Copy, scale=flnw[:, c:c + 1])
                            inst.then_inc(sm.s_hte, 1)
                    if t < T:
                        s.wait_ge(sm.s_fcb, t + 1)
                        s.activation(hf[:, t % NHF, 2048:3072], pFA, AF.Gelu)
                        s.activation(hf[:, t % NHF, 3072:4096], pFB, AF.Gelu
                                     ).then_inc(sm.s_gb, 1)
                    v2 = t + 2
                    if 2 <= v2 < T:
                        s.wait_ge(sm.s_xptr, T + 3 + t)
                        s.activation(xpT[:, v2 % 3, :], pHT, AF.Copy
                                     ).then_inc(sm.s_xpte, 1)
                    w3 = t - 3
                    if 0 <= w3 < T:
                        s.wait_ge(sm.s_nm3, w3 + 1)
                        if w3 >= 2:
                            s.wait_ge(soutL[w3 % 2], 16 * (w3 // 2))
                        s.activation(outb[:, w3 % 2, :], z3[:, w3 % 2, :],
                                     AF.Identity, scale=r3[:, w3 % 2:w3 % 2 + 1],
                                     bias=nm3[:, w3 % 2:w3 % 2 + 1]
                                     ).then_inc(sm.s_out, 1)

            @block.vector
            def _(v):
                def fence(inst):
                    inst.then_inc(sm.s_fv, 1)
                    fv[0] += 1
                    v.wait_ge(sm.s_fv, fv[0])

                def newton_rsqrt(var_ap, out_r, out_nm, mean_ap):
                    fence(v.tensor_scalar(nwv[:], var_ap, LN_EPS, None, ALU.add))
                    fence(v.tensor_scalar(nwi[:], nwv[:].bitcast(I32), 1,
                                          -1, ALU.logical_shift_right,
                                          ALU.bitwise_xor))
                    fence(v.tensor_scalar(nwi[:], nwi[:], 0x5f3759df + 1, None,
                                          ALU.add))
                    fence(v.tensor_copy(nwy[:], nwi[:].bitcast(F32)))
                    fence(v.tensor_scalar(nwh[:], nwv[:], -0.5, None, ALU.mult))
                    for it in range(3):
                        fence(v.tensor_tensor(nwa[:], nwy[:], nwy[:], ALU.mult))
                        fence(v.tensor_scalar(nwb[:], nwa[:], nwh[:], 1.5,
                                              ALU.mult, ALU.add))
                        fence(v.tensor_tensor(out_r if it == 2 else nwy[:],
                                              nwy[:], nwb[:], ALU.mult))
                    fence(v.scalar_tensor_tensor(out_nm, mean_ap, -1.0, out_r,
                                                 ALU.mult, ALU.mult))

                v.wait_ge(sm.s_phA, 5)
                for t in range(T + 5):
                    if t < T:
                        v.wait_ge(sm.s_ga, t + 1)
                        for c in range(4):
                            v.bn_stats(bst2[:, c, :],
                                       hf[:, t % NHF, c * 512:(c + 1) * 512])
                    w = t - 2
                    if 0 <= w < T:
                        v.wait_ge(sm.s_proj, w + 1)
                        v.wait_ge(sxplL[w % 4], 16 * (w // 4 + 1))
                        if w >= 2:
                            v.wait_ge(sm.s_out, w - 1)
                        fence(v.tensor_tensor(z3[:, w % 2, :], pPJ,
                                              xpb[:, w % NXP, :], ALU.add))
                        v.engine_nop().then_inc(sm.s_z3v, 1)
                        v.bn_stats(bst3[:, 0, :], z3[:, w % 2, 0:512])
                        fence(v.bn_stats(bst3[:, 1, :], z3[:, w % 2, 512:1024]))
                        fence(v.bn_aggr(bag3[:], bst3[:]))
                        newton_rsqrt(bag3[:, 1:2], r3[:, w % 2:w % 2 + 1],
                                     nm3[:, w % 2:w % 2 + 1], bag3[:, 0:1])
                        v.engine_nop().then_inc(sm.s_nm3, 1)
                    if t < T:
                        v.wait_ge(sm.s_gb, t + 1)
                        inst = None
                        for c in range(4, 8):
                            inst = v.bn_stats(bst2[:, c, :],
                                              hf[:, t % NHF, c * 512:(c + 1) * 512])
                        fence(inst)
                        fence(v.bn_aggr(bag2[:], bst2[:]))
                        newton_rsqrt(bag2[:, 1:2], r2[:], nm2[:], bag2[:, 0:1])
                        fence(v.tensor_scalar(hf[:, t % NHF, :], hf[:, t % NHF, :],
                                              r2[:], nm2[:], ALU.mult, ALU.add))
                        v.engine_nop().then_inc(sm.s_hl, 1)

            @block.tensor
            def _(te):
                te.wait_ge(sm.s_phA, 5)
                te.wait_ge(sm.s_ve, T)
                te.wait_ge(sm.s_wqf, 16 * 8)
                te.wait_ge(sm.s_z1v, T)
                for v0 in range(2):
                    te.wait_ge(sxplL[v0], 16)
                    if v0 == 1:
                        te.wait_ge(sm.s_xpte, 1)
                    inst = None
                    for c in range(8):
                        inst = te.transpose(pqkv[:, c * P:(c + 1) * P],
                                            xpb[:, v0, c * P:(c + 1) * P],
                                            idn_f[:])
                    inst.then_inc(sm.s_xptr, 1)

                for t in range(T + 5):
                    u = t - 1
                    w = t - 2
                    if t < T:
                        te.wait_ge(sm.s_xpte, t + 1)
                        if t == 0:
                            te.wait_ge(sm.s_yte, T)
                            te.wait_ge(sm.s_xte, T)
                        if t >= 1:
                            te.wait_ge(sm.s_gb, t)
                        inst = None
                        for c in range(8):
                            for piece in range(2):
                                te.matmul(
                                    pFA[:, piece * 512:(piece + 1) * 512],
                                    xpT[:, t % 3, c * P:(c + 1) * P],
                                    wfcq[:, c, piece * 512:(piece + 1) * 512],
                                    start=(c == 0), stop=(c == 7))
                            for piece in range(2):
                                inst = te.matmul(
                                    pFB[:, piece * 512:(piece + 1) * 512],
                                    xpT[:, t % 3, c * P:(c + 1) * P],
                                    wfcq[:, c, 1024 + piece * 512:
                                         1024 + (piece + 1) * 512],
                                    start=(c == 0), stop=(c == 7))
                        inst.then_inc(sm.s_fca, 1)
                    if t == 1:
                        te.wait_ge(sm.s_wql, 16 * NWPP)
                        inst = None
                        for c in range(NWPP):
                            for half in range(2):
                                inst = te.matmul(pqkv[0:1, 1024 + half * 512:
                                                      1024 + (half + 1) * 512],
                                                 flnbb[:, c:c + 1],
                                                 wpq[:, c,
                                                     half * 512:(half + 1) * 512],
                                                 start=(c == 0),
                                                 stop=(c == NWPP - 1))
                            inst.then_inc(sm.s_b2m, 1)
                    if 0 <= u < T or 0 <= w < T:
                        if 0 <= w < T:
                            te.wait_ge(sm.s_hte, 8 * w + 8)
                            if w == 0:
                                te.wait_ge(sm.s_b2e, 1)
                            if w >= 1:
                                te.wait_ge(sm.s_z3v, w)
                            for half in range(2):
                                te.matmul(pPJ[:, half * 512:(half + 1) * 512],
                                          ones_b[:],
                                          bias2[0:1, half * 512:(half + 1) * 512],
                                          start=True, stop=False)
                        if 0 <= u < T:
                            te.wait_ge(sm.s_hl, u + 1)
                            te.wait_ge(sm.s_xpte, min(u + 3, T))
                        pinst = None
                        for r_ in range(8):
                            if 0 <= u < T:
                                if r_ >= 2:
                                    te.wait_ge(sm.s_hte, 8 * u + r_ - 1)
                                inst = None
                                for j in range(4):
                                    col = (4 * r_ + j) * P
                                    inst = te.transpose(
                                        pqkv[:, (r_ % 2) * 512 + j * P:
                                             (r_ % 2) * 512 + (j + 1) * P],
                                        hf[:, u % NHF, col:col + P], idn_f[:])
                                inst.then_inc(sm.s_htr, 1)
                            if 0 <= w < T:
                                for c in range(4 * r_, 4 * r_ + 4):
                                    for half in range(2):
                                        pinst = te.matmul(
                                            pPJ[:, half * 512:(half + 1) * 512],
                                            hT[:, w % 2, c, :],
                                            wpq[:, c, half * 512:(half + 1) * 512],
                                            start=False, stop=(c == NWPP - 1))
                        if 0 <= w < T:
                            pinst.then_inc(sm.s_proj, 1)
                    if t < T:
                        te.wait_ge(sm.s_ga, t + 1)
                        inst = None
                        for c in range(8):
                            for piece in range(2):
                                te.matmul(
                                    pFA[:, piece * 512:(piece + 1) * 512],
                                    xpT[:, t % 3, c * P:(c + 1) * P],
                                    wfcq[:, c, 2048 + piece * 512:
                                         2048 + (piece + 1) * 512],
                                    start=(c == 0), stop=(c == 7))
                            for piece in range(2):
                                inst = te.matmul(
                                    pFB[:, piece * 512:(piece + 1) * 512],
                                    xpT[:, t % 3, c * P:(c + 1) * P],
                                    wfcq[:, c, 3072 + piece * 512:
                                         3072 + (piece + 1) * 512],
                                    start=(c == 0), stop=(c == 7))
                        inst.then_inc(sm.s_fcb, 1)
                    v2 = t + 2
                    if 2 <= v2 < T:
                        te.wait_ge(sxplL[v2 % 4], 16 * (v2 // 4 + 1))
                        if 0 <= u < T:
                            te.wait_ge(sm.s_hte, 8 * u + 8)
                        te.wait_ge(sm.s_xpte, t + 2)
                        inst = None
                        for c in range(8):
                            inst = te.transpose(pqkv[:, c * P:(c + 1) * P],
                                                xpb[:, v2 % NXP, c * P:(c + 1) * P],
                                                idn_f[:])
                        inst.then_inc(sm.s_xptr, 1)

    return nc


_NC_CACHE = [None]


def kernel(L0_attn_x, L0_attn_Wq, L0_attn_Wk, L0_attn_Wv, L0_attn_Wo,
           L0_attn_lambda_q1, L0_attn_lambda_k1, L0_attn_lambda_q2,
           L0_attn_lambda_k2, L0_attn_ln_w, L0_attn_ln_b,
           L0_ffn_W_fc, L0_ffn_ln_w, L0_ffn_ln_b, L0_ffn_W_proj):
    f32 = np.float32
    x = np.asarray(L0_attn_x, f32)
    Wq = np.asarray(L0_attn_Wq, f32)
    Wk = np.asarray(L0_attn_Wk, f32)
    Wv = np.asarray(L0_attn_Wv, f32)
    Wo = np.asarray(L0_attn_Wo, f32)

    def perm_qk(W):
        Wh = W.reshape(D, 16, HD)
        return np.ascontiguousarray(
            np.concatenate([Wh[:, 0::2, :], Wh[:, 1::2, :]], axis=1
                           ).reshape(D, D))

    wqkv = np.ascontiguousarray(
        np.concatenate([perm_qk(Wq), perm_qk(Wk), Wv], axis=1))
    wfct = np.ascontiguousarray(np.asarray(L0_ffn_W_fc, f32).T)
    wpt = np.ascontiguousarray(np.asarray(L0_ffn_W_proj, f32).T)
    lamv = np.ascontiguousarray(np.concatenate([
        np.asarray(L0_attn_lambda_q1, f32), np.asarray(L0_attn_lambda_k1, f32),
        np.asarray(L0_attn_lambda_q2, f32), np.asarray(L0_attn_lambda_k2, f32)])
        [None, :])
    alnw = np.ascontiguousarray(np.tile(np.asarray(L0_attn_ln_w, f32)[None, :],
                                        (P, 1)))
    alnb = np.ascontiguousarray(np.asarray(L0_attn_ln_b, f32)[:, None])
    flnw = np.ascontiguousarray(np.asarray(L0_ffn_ln_w, f32).reshape(NWPP, P).T)
    flnb = np.ascontiguousarray(np.asarray(L0_ffn_ln_b, f32).reshape(NWPP, P).T)

    pidx = np.arange(P)
    mask = ((pidx[:, None] // 8) == (pidx[None, :] // 8)).astype(f32)
    onesc = np.ones((1, P), f32)
    idnc = np.eye(P, dtype=f32)

    if _NC_CACHE[0] is None:
        _NC_CACHE[0] = build_program()
    nc = _NC_CACHE[0]

    in_maps = []
    for i in range(NCORES):
        in_maps.append(dict(
            x=np.ascontiguousarray(x[i]), wqkv=wqkv, wo=np.ascontiguousarray(Wo),
            wfct=wfct, wpt=wpt, lamv=lamv, alnw=alnw, alnb=alnb, flnw=flnw,
            flnb=flnb, maskc=mask, onesc=onesc, idnc=idnc))
    res = run_bass_kernel_spmd(nc, in_maps, core_ids=list(range(NCORES)))
    out = np.stack([res.results[i]["out"] for i in range(NCORES)], axis=0)
    return out.astype(np.float32)
